# version 12
# baseline (speedup 1.0000x reference)
import math
import contextlib
import numpy as np
import ml_dtypes
import concourse.bass as bass
import concourse.mybir as mybir
from concourse.bass_utils import run_bass_kernel_spmd

F32 = mybir.dt.float32
BF16 = mybir.dt.bfloat16
AF = mybir.ActivationFunctionType
ALU = mybir.AluOpType
AX = mybir.AxisListType

S = 2048
DM = 2048
NH = 16
NKV = 4
HD = 128
FH = 8192
NQB = 4
QBT = 512
NTB = 4
TBT = 512
NCH = 16

LN_EPS = 1e-5
QK_EPS = 1e-6
RMS_EPS = 1.1920928955078125e-07
RSCALE = 1.0 / math.sqrt(128.0)
M_TIE = 0.501953125

ENGS = ("sync", "vector", "scalar", "tensor", "gpsimd")


def build(dbg=False):
    nc = bass.Bass("TRN2", target_bir_lowering=False, debug=False,
                   detect_race_conditions=False)

    xT = nc.dram_tensor("xT", [DM, S], F32, kind="ExternalInput").ap()
    xR = nc.dram_tensor("xR", [S, DM], F32, kind="ExternalInput").ap()
    wqkvgT = nc.dram_tensor("wqkvgT", [DM, 5120], F32, kind="ExternalInput").ap()
    woT = nc.dram_tensor("woT", [DM, DM], F32, kind="ExternalInput").ap()
    wguT = nc.dram_tensor("wguT", [DM, 2 * FH], F32, kind="ExternalInput").ap()
    wpT = nc.dram_tensor("wpT", [FH, DM], F32, kind="ExternalInput").ap()
    ropeC2 = nc.dram_tensor("ropeC2", [S, HD], F32, kind="ExternalInput").ap()
    ropeS2 = nc.dram_tensor("ropeS2", [S, HD], F32, kind="ExternalInput").ap()
    qkn = nc.dram_tensor("qkn", [4, 128, HD], F32, kind="ExternalInput").ap()
    identf = nc.dram_tensor("identf", [128, 128], F32, kind="ExternalInput").ap()
    mask01 = nc.dram_tensor("mask01", [4, 128, QBT], BF16, kind="ExternalInput").ap()
    blkones = nc.dram_tensor("blkones", [128, 128], BF16, kind="ExternalInput").ap()

    _scrkind = "ExternalOutput" if dbg else "Internal"
    x1scr = nc.dram_tensor("x1scr", [S, DM], F32, kind=_scrkind).ap()
    wkvgH = nc.dram_tensor("wkvgH", [DM, 5120], BF16, kind="Internal").ap()
    wkvgL = nc.dram_tensor("wkvgL", [DM, 5120], BF16, kind="Internal").ap()
    woH = nc.dram_tensor("woH", [DM, DM], BF16, kind="Internal").ap()
    woL = nc.dram_tensor("woL", [DM, DM], BF16, kind="Internal").ap()
    wqgscr = nc.dram_tensor("wqgscr", [128, 128, DM], BF16, kind=_scrkind).ap()
    wqpscr = nc.dram_tensor("wqpscr", [64, 128, DM], BF16, kind=_scrkind).ap()

    y = nc.dram_tensor("y", [S, DM], F32, kind="ExternalOutput").ap()
    if dbg:
        d_k = nc.dram_tensor("d_k", [2, 128, NKV * S], F32, kind="ExternalOutput").ap()
        d_v = nc.dram_tensor("d_v", [2, 128, NKV * S], F32, kind="ExternalOutput").ap()
        d_q = nc.dram_tensor("d_q", [2, 128, NH * QBT], F32, kind="ExternalOutput").ap()
        d_sg = nc.dram_tensor("d_sg", [128, NH * QBT], F32, kind="ExternalOutput").ap()
        d_g = nc.dram_tensor("d_g", [2, 128, NH * QBT], F32, kind="ExternalOutput").ap()

    prog = {e: [] for e in ENGS}

    def em(eng, fn):
        prog[eng].append(fn)

    es_sem = contextlib.ExitStack()

    class Ct:
        def __init__(self, name):
            self.h = es_sem.enter_context(nc.semaphore(name))
            self.n = 0

    sLD = Ct("sLD")
    sST = Ct("sST")
    sPE = Ct("sPE")
    sDV = Ct("sDV")
    sAC = Ct("sAC")
    sS = [Ct(f"sS{i}") for i in range(4)]
    sT = [Ct(f"sT{i}") for i in range(2)]
    sGP = Ct("sGP")
    ALLS = {"sLD": sLD, "sST": sST, "sPE": sPE, "sDV": sDV, "sAC": sAC,
            "sGP": sGP}
    for i, c in enumerate(sS):
        ALLS[f"sS{i}"] = c
    for i, c in enumerate(sT):
        ALLS[f"sT{i}"] = c
    ECT = {"sync": sLD, "gpsimd": sST, "vector": sDV, "scalar": sAC,
           "tensor": sPE}
    OPCT = dict(ECT, gpsimd=sGP)

    def wait(eng, ct, val):
        if val > 0:
            em(eng, lambda e, h=ct.h, v=val: e.wait_ge(h, v))

    def barrier():
        vals = {k: c.n for k, c in ALLS.items()}
        for eng in ENGS:
            for k, c in ALLS.items():
                wait(eng, c, vals[k])

    def dma(eng, out_ap, in_ap, sem=None):
        ct = sem if sem is not None else ECT[eng]
        em(eng, lambda e, o=out_ap, i=in_ap, h=ct.h:
           e.dma_start(out=o, in_=i).then_inc(h, 16))
        ct.n += 16
        return ct.n

    def op(eng, name, *args, inc=True, **kw):
        ct = OPCT[eng]
        need_drain = False
        if eng in ("vector", "scalar") and args:
            try:
                fb = args[0].free_nbytes()
                el4 = fb == 4 * args[0].free_size()
                need_drain = (fb <= 256 or args[0].partition_size() <= 2
                              or (el4 and fb <= 2048))
            except Exception:
                need_drain = False

        def f(e, name=name, args=args, kw=kw, inc=inc, h=ct.h, dr=need_drain):
            instr = getattr(e, name)(*args, **kw)
            if dr:
                e.drain()
            if inc:
                instr.then_inc(h, 1)
        em(eng, f)
        if inc:
            ct.n += 1
        return ct.n

    def pe_inc():
        em("tensor", lambda e, h=sPE.h: e.sem_inc(h, 1))
        sPE.n += 1
        return sPE.n

    es_glob = contextlib.ExitStack()

    _uid = [0]

    def _un(name):
        _uid[0] += 1
        return f"{name}_{_uid[0]}"

    def sb(es, name, shape, dt, side="left"):
        return es.enter_context(nc.sbuf_tensor(_un(name), shape, dt, side=side))

    def ps(es, name, shape, dt=F32):
        return es.enter_context(nc.psum_tensor(_un(name), shape, dt))

    IDF = sb(es_glob, "IDF", [128, 128], F32)
    BLK = sb(es_glob, "BLK", [128, 128], BF16)
    ONESK = sb(es_glob, "ONESK", [128, 1], BF16)
    ONES1 = sb(es_glob, "ONES1", [1, 128], BF16)
    ONESF = sb(es_glob, "ONESF", [128, 1], F32)

    CBIAS = sb(es_glob, "CBIAS", [128, 4], F32)
    nc.const_aps.register(F32, QK_EPS, CBIAS[:, 0:1])
    nc.const_aps.register(F32, LN_EPS, CBIAS[:, 1:2])
    nc.const_aps.register(F32, RMS_EPS, CBIAS[:, 2:3])

    dma("sync", IDF[:], identf)
    c_consts = dma("sync", BLK[:], blkones)
    op("vector", "memset", ONESK[:], 1.0, inc=False)
    op("vector", "memset", ONES1[:], 1.0, inc=False)
    op("vector", "memset", CBIAS[:, 0:1], QK_EPS, inc=False)
    op("vector", "memset", CBIAS[:, 1:2], LN_EPS, inc=False)
    op("vector", "memset", CBIAS[:, 2:3], RMS_EPS, inc=False)
    c_ones = op("vector", "memset", ONESF[:], 1.0)
    wait("scalar", sDV, c_ones)

    es_q = contextlib.ExitStack()
    QW = sb(es_q, "QW", [128, 2 * DM], F32)
    QA = sb(es_q, "QA", [128, 2 * DM], BF16)
    QB = sb(es_q, "QB", [128, DM], BF16)
    QM = sb(es_q, "QM", [128, DM], BF16)
    QSG = sb(es_q, "QSG", [128, 2 * DM], BF16)
    QT_ = sb(es_q, "QT", [128, DM], BF16)
    QWB = sb(es_q, "QWB", [128, DM], BF16)
    QD = sb(es_q, "QD", [128, DM], BF16)
    QH = sb(es_q, "QH", [128, 2 * DM], BF16)
    es_qp = contextlib.ExitStack()
    QP = [ps(es_qp, f"QP{j}", [128, 512]) for j in range(8)]

    qtiles = [("gu", c, fb) for c in range(NCH) for fb in range(8)]
    qtiles += [("p", c, 0) for c in range(64)]
    qtiles += [("kvg", c, fb) for c in range(NCH) for fb in range(3)]
    qtiles += [("wo", c, 0) for c in range(NCH)]

    q_ld, q_ab, q_mm, q_ev, q_st = {}, {}, {}, {}, {}
    q_free = {}
    wait("tensor", sLD, c_consts)
    for i, (kind, c, fb) in enumerate(qtiles):
        sl = i % 2
        if i >= 2:
            ctf, vf = q_free[i - 2]
            wait("sync", ctf, vf)
        if kind == "gu":
            srcw = wguT[c * 128:(c + 1) * 128, fb * DM:(fb + 1) * DM]
            ncol = DM
        elif kind == "p":
            srcw = wpT[c * 128:(c + 1) * 128, 0:DM]
            ncol = DM
        elif kind == "kvg":
            w0 = fb * 1792
            ncol = min(1792, 5120 - w0)
            srcw = wqkvgT[c * 128:(c + 1) * 128, w0:w0 + ncol]
        else:
            srcw = woT[c * 128:(c + 1) * 128, 0:DM]
            ncol = DM
        q_ld[i] = dma("sync", QW[:, sl * DM: sl * DM + ncol], srcw, sem=sS[sl])
        if kind in ("kvg", "wo"):
            wait("vector", sS[sl], q_ld[i])
            if i >= 2:
                wait("vector", sT[sl], q_st[i - 2])
            op("vector", "tensor_copy", QA[:, sl * DM: sl * DM + ncol],
               QW[:, sl * DM: sl * DM + ncol], inc=False)
            q_ev[i] = op("vector", "scalar_tensor_tensor",
                         QSG[:, sl * DM: sl * DM + ncol],
                         QA[:, sl * DM: sl * DM + ncol], -1.0,
                         QW[:, sl * DM: sl * DM + ncol], ALU.mult, ALU.add)
            q_free[i] = (sDV, q_ev[i])
            wait("gpsimd", sDV, q_ev[i])
            if kind == "kvg":
                dh = wkvgH[c * 128:(c + 1) * 128, w0:w0 + ncol]
                dl = wkvgL[c * 128:(c + 1) * 128, w0:w0 + ncol]
            else:
                dh = woH[c * 128:(c + 1) * 128, :]
                dl = woL[c * 128:(c + 1) * 128, :]
            dma("gpsimd", dh, QA[:, sl * DM: sl * DM + ncol], sem=sT[sl])
            q_st[i] = dma("gpsimd", dl, QSG[:, sl * DM: sl * DM + ncol],
                          sem=sT[sl])
            continue
        wait("scalar", sS[sl], q_ld[i])
        if i >= 2:
            wait("scalar", sPE, q_mm.get(i - 2, 0))
            wait("scalar", sDV, q_ev[i - 2])
        op("scalar", "activation", QA[:, sl * DM:(sl + 1) * DM],
           QW[:, sl * DM:(sl + 1) * DM], AF.Abs, inc=False)
        q_ab[i] = op("scalar", "activation", QSG[:, sl * DM:(sl + 1) * DM],
                     QW[:, sl * DM:(sl + 1) * DM], AF.Sign)
        q_free[i] = (sAC, q_ab[i])
        wait("tensor", sAC, q_ab[i])
        if i >= 2:
            wait("tensor", sDV, q_ev[i - 2])
        for j in range(4):
            q_mm[i] = op("tensor", "matmul", QP[(sl * 4) + j][:], BLK[:],
                         QA[:, sl * DM + j * 512: sl * DM + (j + 1) * 512],
                         start=True, stop=True, inc=(j == 3))
        wait("vector", sPE, q_mm[i])
        if i >= 2:
            wait("vector", sT[sl], q_st[i - 2])
        for j in range(4):
            op("vector", "tensor_scalar", QB[:, j * 512:(j + 1) * 512],
               QP[(sl * 4) + j][:], 0.015625, 1e-8, ALU.mult, ALU.max,
               inc=False)
        prev = getattr(build, "_q_prev", None)
        if prev is not None:
            wait("vector", sGP, prev)
        op("vector", "scalar_tensor_tensor", QM[:], QB[:], M_TIE,
           QA[:, sl * DM:(sl + 1) * DM], ALU.mult, ALU.is_lt, inc=False)
        tdone = op("vector", "tensor_tensor", QT_[:], QM[:], QB[:], ALU.mult)
        wait("gpsimd", sDV, tdone)
        if prev is not None:
            wait("gpsimd", sDV, q_ev[i - 1])
        op("gpsimd", "tensor_tensor", QD[:], QT_[:],
           QA[:, sl * DM:(sl + 1) * DM], ALU.subtract, inc=False)
        vdone = op("gpsimd", "tensor_tensor", QWB[:],
                   QA[:, sl * DM:(sl + 1) * DM], QD[:], ALU.add)
        build._q_prev = vdone
        wait("vector", sGP, vdone)
        q_ev[i] = op("vector", "tensor_tensor", QH[:, sl * DM:(sl + 1) * DM],
                     QWB[:], QSG[:, sl * DM:(sl + 1) * DM], ALU.mult)
        wait("gpsimd", sDV, q_ev[i])
        if kind == "gu":
            dst = wqgscr[fb * 16:(fb + 1) * 16, :, c * 128:(c + 1) * 128]
            dst = dst.rearrange("hp p f -> p hp f")
            srcap = QH[:, sl * DM:(sl + 1) * DM].rearrange(
                "p (hp f) -> p hp f", hp=16)
        else:
            dst = wqpscr[c, :, :]
            srcap = QH[:, sl * DM:(sl + 1) * DM]
        q_st[i] = dma("gpsimd", dst, srcap, sem=sT[sl])

    barrier()
    es_qp.close()
    es_q.close()

    es_a = contextlib.ExitStack()
    KH = sb(es_a, "KH", [128, NKV * S], BF16)
    KL = sb(es_a, "KL", [128, NKV * S], BF16)
    VH = sb(es_a, "VH", [128, NKV * S], BF16)
    VL = sb(es_a, "VL", [128, NKV * S], BF16)
    QKN = sb(es_a, "QKN", [128, 4 * HD], F32)
    MSK = sb(es_a, "MSK", [128, 4 * QBT], BF16)
    PCA = sb(es_a, "PCA", [128, 512], F32)
    TF0 = sb(es_a, "TF0", [128, 512], F32)
    TF1 = sb(es_a, "TF1", [128, 512], F32)
    TF2 = sb(es_a, "TF2", [128, 512], F32)
    TS4 = sb(es_a, "TS4", [128, 64], F32)

    dma("sync", QKN[:].rearrange("p (j f) -> p j f", j=4),
        qkn.rearrange("j p f -> p j f"))
    c_ac = dma("sync", MSK[:].rearrange("p (j f) -> p j f", j=4),
               mask01.rearrange("j p f -> p j f"))
    wait("vector", sLD, c_ac)
    wait("tensor", sLD, c_ac)
    wait("tensor", sDV, c_ones)

    FB_OFF = [0, 512, 1024, 1536, 2048, 2560, 3072, 3584, 4096, 4608]
    FB_KIND = ["q", "q", "q", "q", "g", "g", "g", "g", "k", "v"]

    class PairStream:

        def __init__(self, bh, bl, srch, srcl, sems):
            self.i = 0
            self.rdy = {}
            self.used = {}
            self.bh, self.bl = bh, bl
            self.srch, self.srcl = srch, srcl
            self.sems = sems

        def load(self, r0, c0, c1):
            bh, bl = self.bh, self.bl
            i = self.i
            self.i += 1
            sl = i % 2
            sem = self.sems[sl]
            if i >= 2:
                wait("sync", sPE, self.used[i - 2])
            w = c1 - c0
            srch = self.srch[r0:r0 + 512, c0:c1].rearrange(
                "(c p) f -> p c f", p=128)
            srcl = self.srcl[r0:r0 + 512, c0:c1].rearrange(
                "(c p) f -> p c f", p=128)
            dsth = bh[:, sl * 2048: sl * 2048 + 4 * w].rearrange(
                "p (c f) -> p c f", c=4)
            dstl = bl[:, sl * 2048: sl * 2048 + 4 * w].rearrange(
                "p (c f) -> p c f", c=4)
            dma("sync", dsth, srch, sem=sem)
            self.rdy[i] = dma("sync", dstl, srcl, sem=sem)
            return i, sl

        def chunk(self, sl, cc, w=512):
            return (self.bh[:, sl * 2048 + cc * w: sl * 2048 + (cc + 1) * w],
                    self.bl[:, sl * 2048 + cc * w: sl * 2048 + (cc + 1) * w])

        def consumed(self, i):
            self.used[i] = sPE.n

    for qb in range(NQB):
        es_l = contextlib.ExitStack()
        QTH = sb(es_l, "QTH", [128, NH * QBT], BF16)
        QTL = sb(es_l, "QTL", [128, NH * QBT], BF16)
        SGATE = sb(es_l, "SGATE", [128, NH * QBT], F32)
        es_p1 = contextlib.ExitStack()
        PW = [ps(es_p1, f"PW{j}", [128, 512]) for j in range(8)]
        es_a1 = contextlib.ExitStack()
        WSH = sb(es_a1, "WSH", [128, 2 * 2048], BF16)
        WSL = sb(es_a1, "WSL", [128, 2 * 2048], BF16)
        XT32 = sb(es_a1, "XT32", [128, 2048], F32)
        XTH = sb(es_a1, "XTH", [128, NCH * QBT], BF16)
        XTL = sb(es_a1, "XTL", [128, NCH * QBT], BF16)
        C2 = sb(es_a1, "C2", [128, 4 * HD], F32)
        S2T = sb(es_a1, "S2T", [128, 4 * HD], F32)
        XRP = sb(es_a1, "XRP", [128, 512], F32)

        dma("sync", C2[:].rearrange("p (tt f) -> p tt f", tt=4),
            ropeC2.rearrange("(tt p) f -> p tt f", p=128)[:, qb * 4:qb * 4 + 4, :])
        c_rope = dma("sync", S2T[:].rearrange("p (tt f) -> p tt f", tt=4),
                     ropeS2.rearrange("(tt p) f -> p tt f", p=128)[:, qb * 4:qb * 4 + 4, :])
        wait("vector", sLD, c_rope)

        ws = PairStream(WSH, WSL, wkvgH, wkvgL, (sS[0], sS[1]))
        xt_split = {}
        for cg in range(4):
            if cg >= 1:
                wait("sync", sDV, xt_split[cg - 1])
            srcx = xT[cg * 512:(cg + 1) * 512,
                      qb * QBT:(qb + 1) * QBT].rearrange(
                          "(c p) f -> p c f", p=128)
            dstx = XT32[:].rearrange("p (c f) -> p c f", c=4)
            t = dma("sync", dstx, srcx, sem=sS[2 + cg % 2])
            wait("vector", sS[2 + cg % 2], t)
            op("vector", "tensor_copy",
               XTH[:, cg * 4 * QBT:(cg + 1) * 4 * QBT],
               XT32[:], inc=False)
            xt_split[cg] = op("vector", "scalar_tensor_tensor",
                              XTL[:, cg * 4 * QBT:(cg + 1) * 4 * QBT],
                              XTH[:, cg * 4 * QBT:(cg + 1) * 4 * QBT], -1.0,
                              XT32[:], ALU.mult, ALU.add)
        xt_ready = xt_split[3]
        wave_free = [0, 0]
        grp_done = {}
        pend_evac = []

        def emit_mms(fb):
            kind = FB_KIND[fb]
            w = fb % 2
            banks = PW[w * 4:(w + 1) * 4]
            for c in range(NCH):
                cc = c % 4
                if cc == 0:
                    i, sl = ws.load(c * 128, FB_OFF[fb], FB_OFF[fb] + 512)
                    wait("tensor", ws.sems[sl], ws.rdy[i])
                if c == 0:
                    wait("tensor", sDV, wave_free[w])
                    wait("tensor", sDV, xt_ready)
                wsh, wsl = ws.chunk(sl, cc)
                xth_f = XTH[:, c * QBT:(c + 1) * QBT]
                xtl_f = XTL[:, c * QBT:(c + 1) * QBT]
                if kind == "g":
                    for fs in range(4):
                        op("tensor", "matmul", banks[fs][:],
                           wsh[:, fs * 128:(fs + 1) * 128], xth_f,
                           start=(c == 0), stop=False, inc=False)
                        op("tensor", "matmul", banks[fs][:],
                           wsh[:, fs * 128:(fs + 1) * 128], xtl_f,
                           start=False, stop=False, inc=False)
                        op("tensor", "matmul", banks[fs][:],
                           wsl[:, fs * 128:(fs + 1) * 128], xth_f,
                           start=False, stop=(c == NCH - 1), inc=(fs == 3))
                else:
                    for s in range(4):
                        xth = xth_f[:, s * 128:(s + 1) * 128]
                        xtl = xtl_f[:, s * 128:(s + 1) * 128]
                        op("tensor", "matmul", banks[s][:], xth, wsh,
                           start=(c == 0), stop=False, inc=False)
                        op("tensor", "matmul", banks[s][:], xth, wsl,
                           start=False, stop=False, inc=False)
                        op("tensor", "matmul", banks[s][:], xtl, wsh,
                           start=False, stop=(c == NCH - 1), inc=(s == 3))
                if cc == 3:
                    ws.consumed(i)
            grp_done[fb] = sPE.n

        def emit_evac(fb):
            kind = FB_KIND[fb]
            w = fb % 2
            banks = PW[w * 4:(w + 1) * 4]
            gd = grp_done[fb]
            last = 0
            if kind == "g":
                sgx_read = getattr(emit_evac, "_sgx_read", [0, 0])
                for fs in range(4):
                    hh = (fb - 4) * 4 + fs
                    wait("scalar", sPE, gd)
                    wait("scalar", sDV,
                         max(sgx_read[0], getattr(emit_evac, "_sqg", 0)))
                    ac = op("scalar", "activation",
                            TF2[:], banks[fs][:], AF.Exp, scale=-1.0)
                    wait("vector", sAC, ac)
                    sgx_read[0] = op("vector", "tensor_scalar", TF0[:],
                                     TF2[:], 1.0, None, ALU.add)
                    last = op("vector", "reciprocal",
                              SGATE[:, hh * QBT:(hh + 1) * QBT], TF0[:])
                emit_evac._sgx_read = sgx_read
            elif kind == "v":
                vh = VH[:].rearrange("p (kv tt f) -> p kv tt f", kv=NKV, tt=16)
                vl = VL[:].rearrange("p (kv tt f) -> p kv tt f", kv=NKV, tt=16)
                for s in range(4):
                    tt = qb * 4 + s
                    wait("vector", sPE, gd)
                    op("vector", "tensor_copy", PCA[:], banks[s][:],
                       inc=False)
                    srcv = PCA[:].rearrange("p (kv f) -> p kv f", kv=NKV)
                    op("vector", "tensor_copy", vh[:, :, tt, :], srcv,
                       inc=False)
                    last = op("vector", "scalar_tensor_tensor",
                              vl[:, :, tt, :], vh[:, :, tt, :], -1.0, srcv,
                              ALU.mult, ALU.add)
            else:
                goff = 0 if kind == "q" else 2
                gam = QKN[:, goff * HD:(goff + 1) * HD]
                bet = QKN[:, (goff + 1) * HD:(goff + 2) * HD]
                for s in range(4):
                    tt = qb * 4 + s
                    bank = banks[s]
                    xsl = s % 2
                    sq_guard = max(getattr(emit_evac, "_sqg", 0),
                                   getattr(emit_evac, "_sgx_read", [0, 0])[0])
                    wait("vector", sPE, gd)
                    pcd = op("vector", "tensor_copy", PCA[:], bank[:])
                    wait("scalar", sDV, pcd)
                    wait("scalar", sDV, sq_guard)
                    acq = op("scalar", "activation", TF2[:], PCA[:],
                             AF.Square)
                    wait("vector", sAC, acq)
                    op("vector", "tensor_reduce", TS4[:, 0:4],
                       PCA[:].rearrange("p (g n) -> p g n", g=4), AX.X,
                       ALU.add, inc=False)
                    op("vector", "tensor_scalar", TS4[:, 4:8], TS4[:, 0:4],
                       1.0 / 128.0, None, ALU.mult, inc=False)
                    op("vector", "tensor_reduce", TS4[:, 8:12],
                       TF2[:].rearrange("p (g n) -> p g n", g=4), AX.X,
                       ALU.add, inc=False)
                    op("vector", "tensor_scalar", TS4[:, 12:16], TS4[:, 8:12],
                       1.0 / 128.0, None, ALU.mult, inc=False)
                    op("vector", "tensor_tensor", TS4[:, 16:20], TS4[:, 4:8],
                       TS4[:, 4:8], ALU.mult, inc=False)
                    dv = op("vector", "tensor_tensor", TS4[:, 20:24],
                            TS4[:, 12:16], TS4[:, 16:20], ALU.subtract)
                    wait("scalar", sDV, dv)
                    op("scalar", "activation", TS4[:, 24:28], TS4[:, 20:24],
                       AF.Ln, bias=QK_EPS, inc=False)
                    ac = op("scalar", "activation", TS4[:, 28:32],
                            TS4[:, 24:28], AF.Exp, scale=-0.5)
                    wait("vector", sAC, ac)
                    pcv = PCA[:].rearrange("p (g f) -> p g f", g=4)
                    op("vector", "tensor_tensor",
                       TF1[:].rearrange("p (g f) -> p g f", g=4), pcv,
                       TS4[:, 4:8].unsqueeze(2).broadcast_to((128, 4, HD)),
                       ALU.subtract, inc=False)
                    op("vector", "tensor_tensor",
                       TF1[:].rearrange("p (g f) -> p g f", g=4),
                       TF1[:].rearrange("p (g f) -> p g f", g=4),
                       TS4[:, 28:32].unsqueeze(2).broadcast_to((128, 4, HD)),
                       ALU.mult, inc=False)
                    t1v = TF1[:].rearrange("p (g f) -> p g f", g=4)
                    t2v = TF2[:].rearrange("p (g f) -> p g f", g=4)
                    op("vector", "tensor_tensor", t2v, t1v,
                       gam.unsqueeze(1).broadcast_to((128, 4, HD)), ALU.mult,
                       inc=False)
                    op("vector", "tensor_tensor", t1v, t2v,
                       bet.unsqueeze(1).broadcast_to((128, 4, HD)), ALU.add,
                       inc=False)
                    if True:
                        pass
                    xrp_tr = getattr(emit_evac, "_xrp_tr", [0, 0])
                    wait("vector", sPE, xrp_tr[0])
                    xr = XRP[:]
                    c2b = C2[:, s * HD:(s + 1) * HD].unsqueeze(1)
                    c2b = c2b.broadcast_to((128, 4, HD))
                    s2b = S2T[:, s * HD:(s + 1) * HD].rearrange(
                        "p (pr two) -> p pr two", two=2).unsqueeze(1)
                    s2b = s2b.broadcast_to((128, 4, 64, 2))
                    op("vector", "tensor_tensor", TF0[:].rearrange(
                        "p (g f) -> p g f", g=4), t1v, c2b, ALU.mult,
                       inc=False)
                    swap = TF1[:].rearrange("p (g pr two) -> p g pr two",
                                            g=4, two=2)[:, :, :, ::-1]
                    op("vector", "tensor_tensor", TF2[:].rearrange(
                        "p (g pr two) -> p g pr two", g=4, two=2), swap, s2b,
                       ALU.mult, inc=False)
                    rp = op("vector", "tensor_tensor", xr, TF0[:], TF2[:],
                            ALU.add)
                    emit_evac._sqg = rp
                    wait("tensor", sDV, rp)
                    for g in range(4):
                        op("tensor", "transpose",
                           bank[:, g * 128:(g + 1) * 128],
                           xr[:, g * 128:(g + 1) * 128], IDF[:],
                           inc=(g == 3))
                    trd = sPE.n
                    xrp_tr[0] = trd
                    emit_evac._xrp_tr = xrp_tr
                    wait("vector", sPE, trd)
                    if kind == "q":
                        dsth = QTH[:].rearrange("p (hh t) -> p hh t", hh=NH)
                        dstl = QTL[:].rearrange("p (hh t) -> p hh t", hh=NH)
                        dh = dsth[:, fb * 4:fb * 4 + 4, s * 128:(s + 1) * 128]
                        dl = dstl[:, fb * 4:fb * 4 + 4, s * 128:(s + 1) * 128]
                    else:
                        dsth = KH[:].rearrange("p (kv t) -> p kv t", kv=NKV)
                        dstl = KL[:].rearrange("p (kv t) -> p kv t", kv=NKV)
                        dh = dsth[:, :, qb * QBT + s * 128:
                                  qb * QBT + (s + 1) * 128]
                        dl = dstl[:, :, qb * QBT + s * 128:
                                  qb * QBT + (s + 1) * 128]
                    op("vector", "tensor_copy", PCA[:], bank[:], inc=False)
                    srcv = PCA[:].rearrange("p (g f) -> p g f", g=4)
                    op("vector", "tensor_copy", dh, srcv, inc=False)
                    last = op("vector", "scalar_tensor_tensor", dl, dh, -1.0,
                              srcv, ALU.mult, ALU.add)
            wave_free[w] = last

        for fb in range(10):
            emit_mms(fb)
            if fb >= 1:
                emit_evac(fb - 1)
        emit_evac(9)

        barrier()
        if dbg and qb == 0:
            dma("gpsimd", d_k[0], KH[:])
            dma("gpsimd", d_k[1], KL[:])
            dma("gpsimd", d_v[0], VH[:])
            dma("gpsimd", d_v[1], VL[:])
            dma("gpsimd", d_q[0], QTH[:])
            dma("gpsimd", d_q[1], QTL[:])
            dma("gpsimd", d_sg, SGATE[:])
            wait("gpsimd", sST, sST.n)
            barrier()
        es_p1.close()
        es_a1.close()

        es_r2 = contextlib.ExitStack()
        GH = sb(es_r2, "GH", [128, NH * QBT], BF16, side="right")
        GL = sb(es_r2, "GL", [128, NH * QBT], BF16, side="right")
        EF = sb(es_r2, "EF", [128, 2 * 512], F32, side="right")
        EHB = sb(es_r2, "EHB", [128, 2 * 512], BF16, side="right")
        ELB = sb(es_r2, "ELB", [128, 2 * 512], BF16, side="right")
        RD = sb(es_r2, "RD", [1, 512], F32, side="right")
        RDH = sb(es_r2, "RDH", [1, 512], BF16, side="right")
        RDL = sb(es_r2, "RDL", [1, 512], BF16, side="right")
        RB = sb(es_r2, "RB", [128, 512], F32, side="right")
        es_p2 = contextlib.ExitStack()
        PS_ = [ps(es_p2, "PSa", [128, 512]), ps(es_p2, "PSb", [128, 512])]
        PAV = ps(es_p2, "PAV", [128, 512])
        PDN = ps(es_p2, "PDN", [1, 512])
        PRB = ps(es_p2, "PRB", [128, 512])

        nk = 4 * (qb + 1)
        gate_done = 0
        for h in range(NH):
            kvh = h // 4
            qh_ap = QTH[:, h * QBT:(h + 1) * QBT]
            ql_ap = QTL[:, h * QBT:(h + 1) * QBT]
            exp_done, el_done, av_done = {}, {}, {}

            def emit_S(kc):
                sbk = kc % 2
                if kc >= 2:
                    wait("tensor", sAC, exp_done[kc - 2])
                kh = KH[:, kvh * S + kc * 128: kvh * S + (kc + 1) * 128]
                kl = KL[:, kvh * S + kc * 128: kvh * S + (kc + 1) * 128]
                op("tensor", "matmul", PS_[sbk][:], kh, qh_ap,
                   start=True, stop=False, inc=False)
                op("tensor", "matmul", PS_[sbk][:], kh, ql_ap,
                   start=False, stop=False, inc=False)
                op("tensor", "matmul", PS_[sbk][:], kl, qh_ap,
                   start=False, stop=True, inc=True)
                wait("scalar", sPE, sPE.n)
                if kc >= 2:
                    wait("scalar", sDV, el_done[kc - 2])
                exp_done[kc] = op("scalar", "activation",
                                  EF[:, sbk * 512:(sbk + 1) * 512],
                                  PS_[sbk][:], AF.Exp, scale=RSCALE)
                wait("vector", sAC, exp_done[kc])
                if kc >= 2:
                    wait("vector", sPE, av_done[kc - 2])
                efs = EF[:, sbk * 512:(sbk + 1) * 512]
                if kc >= 4 * qb:
                    j = kc - 4 * qb
                    op("vector", "tensor_tensor", efs, efs,
                       MSK[:, j * QBT:(j + 1) * QBT], ALU.mult, inc=False)
                op("vector", "tensor_copy",
                   EHB[:, sbk * 512:(sbk + 1) * 512], efs, inc=False)
                el_done[kc] = op("vector", "scalar_tensor_tensor",
                                 ELB[:, sbk * 512:(sbk + 1) * 512],
                                 EHB[:, sbk * 512:(sbk + 1) * 512], -1.0, efs,
                                 ALU.mult, ALU.add)

            def emit_AV(kc):
                sbk = kc % 2
                wait("tensor", sDV, el_done[kc])
                if kc == 0 and gate_done:
                    wait("tensor", sDV, gate_done)
                vh = VH[:, (kvh * 16 + kc) * 128:(kvh * 16 + kc + 1) * 128]
                vl = VL[:, (kvh * 16 + kc) * 128:(kvh * 16 + kc + 1) * 128]
                ehs = EHB[:, sbk * 512:(sbk + 1) * 512]
                els = ELB[:, sbk * 512:(sbk + 1) * 512]
                op("tensor", "matmul", PAV[:], vh, ehs,
                   start=(kc == 0), stop=False, inc=False)
                op("tensor", "matmul", PAV[:], vh, els,
                   start=False, stop=False, inc=False)
                op("tensor", "matmul", PAV[:], vl, ehs,
                   start=False, stop=(kc == nk - 1), inc=False)
                op("tensor", "matmul", PDN[:], ONESK[:], ehs,
                   start=(kc == 0), stop=False, inc=False)
                op("tensor", "matmul", PDN[:], ONESK[:], els,
                   start=False, stop=(kc == nk - 1), inc=True)
                av_done[kc] = sPE.n

            emit_S(0)
            for kc in range(nk):
                if kc + 1 < nk:
                    emit_S(kc + 1)
                emit_AV(kc)
            wait("vector", sPE, av_done[nk - 1])
            op("vector", "reciprocal", RD[:], PDN[:], inc=False)
            op("vector", "tensor_copy", RDH[:], RD[:], inc=False)
            dvv = op("vector", "scalar_tensor_tensor", RDL[:], RDH[:], -1.0,
                     RD[:], ALU.mult, ALU.add)
            wait("tensor", sDV, dvv)
            op("tensor", "matmul", PRB[:], ONES1[:], RDH[:],
               start=True, stop=False, inc=False)
            op("tensor", "matmul", PRB[:], ONES1[:], RDL[:],
               start=False, stop=True, inc=True)
            wait("vector", sPE, sPE.n)
            op("vector", "tensor_copy", RB[:], PRB[:], inc=False)
            op("vector", "tensor_tensor", TF0[:], PAV[:], RB[:], ALU.mult,
               inc=False)
            op("vector", "tensor_tensor", TF1[:], TF0[:],
               SGATE[:, h * QBT:(h + 1) * QBT], ALU.mult, inc=False)
            op("vector", "tensor_copy", GH[:, h * QBT:(h + 1) * QBT], TF1[:],
               inc=False)
            gate_done = op("vector", "scalar_tensor_tensor",
                           GL[:, h * QBT:(h + 1) * QBT],
                           GH[:, h * QBT:(h + 1) * QBT], -1.0, TF1[:],
                           ALU.mult, ALU.add)

        barrier()
        if dbg and qb == 0:
            dma("gpsimd", d_g[0], GH[:])
            dma("gpsimd", d_g[1], GL[:])
            wait("gpsimd", sST, sST.n)
            barrier()
        es_p2.close()
        es_l.close()

        es_l2 = contextlib.ExitStack()
        X1Q = sb(es_l2, "X1Q", [128, 4 * DM], F32)
        WSH = sb(es_l2, "WSHb", [128, 2 * 2048], BF16)
        WSL = sb(es_l2, "WSLb", [128, 2 * 2048], BF16)
        XRES = sb(es_l2, "XRES", [128, 2 * 512], F32)
        X1N = sb(es_l2, "X1N", [128, DM], F32)
        es_p3 = contextlib.ExitStack()
        PWO = [ps(es_p3, f"PWO{j}", [128, 512]) for j in range(8)]
        ws = PairStream(WSH, WSL, woH, woL, (sS[0], sS[1]))
        wo_wave_free = [0, 0]
        wo_grp = {}
        xres_used = [0, 0]

        def emit_wo_mms(db):
            w = db % 2
            banks = PWO[w * 4:(w + 1) * 4]
            for fc in range(NCH):
                cc = fc % 4
                if cc == 0:
                    i, sl = ws.load(fc * 128, db * 512, (db + 1) * 512)
                    wait("tensor", ws.sems[sl], ws.rdy[i])
                if fc == 0:
                    wait("tensor", sDV, wo_wave_free[w])
                wsh, wsl = ws.chunk(sl, cc)
                for s in range(4):
                    gh = GH[:, fc * QBT + s * 128: fc * QBT + (s + 1) * 128]
                    gl = GL[:, fc * QBT + s * 128: fc * QBT + (s + 1) * 128]
                    op("tensor", "matmul", banks[s][:], gh, wsh,
                       start=(fc == 0), stop=False, inc=False)
                    op("tensor", "matmul", banks[s][:], gh, wsl,
                       start=False, stop=False, inc=False)
                    op("tensor", "matmul", banks[s][:], gl, wsh,
                       start=False, stop=(fc == NCH - 1), inc=(s == 3))
                if cc == 3:
                    ws.consumed(i)
            wo_grp[db] = sPE.n

        def emit_wo_evac(db):
            nonlocal wo_wave_free, xres_used
            w = db % 2
            banks = PWO[w * 4:(w + 1) * 4]
            for s in range(4):
                sl = (db * 4 + s) % 2
                wait("sync", sDV, xres_used[sl])
                t = dma("sync", XRES[:, sl * 512:(sl + 1) * 512],
                        xR[qb * QBT + s * 128: qb * QBT + (s + 1) * 128,
                           db * 512:(db + 1) * 512], sem=sS[2 + sl])
                wait("vector", sS[2 + sl], t)
                wait("vector", sPE, wo_grp[db])
                ev = op("vector", "tensor_tensor",
                        X1Q[:, s * DM + db * 512: s * DM + (db + 1) * 512],
                        banks[s][:], XRES[:, sl * 512:(sl + 1) * 512],
                        ALU.add)
                xres_used[sl] = ev
            wo_wave_free[w] = ev

        for db in range(4):
            emit_wo_mms(db)
            if db >= 1:
                emit_wo_evac(db - 1)
        emit_wo_evac(3)

        x1_st = {}
        for s in range(4):
            xq = X1Q[:, s * DM:(s + 1) * DM]
            op("vector", "tensor_reduce", TS4[:, 0:1], xq.unsqueeze(1),
               AX.X, ALU.add, inc=False)
            op("vector", "tensor_scalar", TS4[:, 1:2], TS4[:, 0:1],
               1.0 / DM, None, ALU.mult, inc=False)
            for jj in range(4):
                op("vector", "scalar_tensor_tensor", TF0[:],
                   xq[:, jj * 512:(jj + 1) * 512], 1.0,
                   xq[:, jj * 512:(jj + 1) * 512], ALU.mult, ALU.mult,
                   accum_out=TS4[:, 2 + jj:3 + jj], inc=False)
            op("vector", "tensor_reduce", TS4[:, 6:7],
               TS4[:, 2:6].unsqueeze(1), AX.X, ALU.add, inc=False)
            op("vector", "tensor_scalar", TS4[:, 7:8], TS4[:, 6:7],
               1.0 / DM, None, ALU.mult, inc=False)
            op("vector", "tensor_tensor", TS4[:, 8:9], TS4[:, 1:2],
               TS4[:, 1:2], ALU.mult, inc=False)
            dv = op("vector", "tensor_tensor", TS4[:, 9:10], TS4[:, 7:8],
                    TS4[:, 8:9], ALU.subtract)
            wait("scalar", sDV, dv)
            op("scalar", "activation", TS4[:, 10:11], TS4[:, 9:10],
               AF.Ln, bias=LN_EPS, inc=False)
            ac = op("scalar", "activation", TS4[:, 11:12], TS4[:, 10:11],
                    AF.Exp, scale=-0.5)
            wait("vector", sAC, ac)
            if s >= 1:
                wait("vector", sST, x1_st[s - 1])
            dv = op("vector", "tensor_scalar", X1N[:],
                    xq, TS4[:, 1:2], TS4[:, 11:12], ALU.subtract, ALU.mult)
            wait("gpsimd", sDV, dv)
            x1_st[s] = dma(
                "gpsimd",
                x1scr[qb * QBT + s * 128: qb * QBT + (s + 1) * 128, :],
                X1N[:])
        barrier()
        es_p3.close()
        es_l2.close()
        es_r2.close()

    barrier()
    es_a.close()

    es_f = contextlib.ExitStack()
    TF0f = sb(es_f, "TF0f", [128, 2 * 512], F32)
    TF1f = sb(es_f, "TF1f", [128, 512], F32)
    TF2f = sb(es_f, "TF2f", [128, 2 * 512], F32)
    TS4f = sb(es_f, "TS4f", [128, 64], F32)
    RSTB = sb(es_f, "RSTB", [128, 4], F32)
    LNR = sb(es_f, "LNR", [1, 2 * 512], F32)
    HH = sb(es_f, "HH", [128, 64 * TBT], BF16)
    HL = sb(es_f, "HL", [128, 64 * TBT], BF16)

    for tb in range(NTB):
        es_x = contextlib.ExitStack()
        X1TH = sb(es_x, "X1TH", [128, NCH * TBT], BF16, side="right")
        X1TL = sb(es_x, "X1TL", [128, NCH * TBT], BF16, side="right")
        X1LD = sb(es_x, "X1LD", [128, 2 * DM], F32, side="right")
        es_pt = contextlib.ExitStack()
        PTR = [ps(es_pt, f"PTR{j}", [128, 512]) for j in range(2)]
        tr_ev = [0, 0]
        s_trdone = {}
        for s in range(4):
            sl = s % 2
            if s >= 2:
                wait("sync", sPE, s_trdone[s - 2])
            t = dma("sync", X1LD[:, sl * DM:(sl + 1) * DM],
                    x1scr[tb * QBT + s * 128: tb * QBT + (s + 1) * 128, :],
                    sem=sS[sl])
            for grp in range(4):
                bk = grp % 2
                bank = PTR[bk]
                wait("tensor", sS[sl], t)
                wait("tensor", sDV, tr_ev[bk])
                for g in range(4):
                    c = grp * 4 + g
                    op("tensor", "transpose", bank[:, g * 128:(g + 1) * 128],
                       X1LD[:, sl * DM + c * 128: sl * DM + (c + 1) * 128],
                       IDF[:], inc=(g == 3))
                s_trdone[s] = sPE.n
                wait("vector", sPE, sPE.n)
                dsth = X1TH[:].rearrange("p (c t) -> p c t", c=NCH)
                dstl = X1TL[:].rearrange("p (c t) -> p c t", c=NCH)
                dh = dsth[:, grp * 4:(grp + 1) * 4, s * 128:(s + 1) * 128]
                dl = dstl[:, grp * 4:(grp + 1) * 4, s * 128:(s + 1) * 128]
                op("vector", "tensor_copy", TF0f[:, 0:512], bank[:],
                   inc=False)
                srcv = TF0f[:, 0:512].rearrange("p (g f) -> p g f", g=4)
                op("vector", "tensor_copy", dh, srcv, inc=False)
                tr_ev[bk] = op("vector", "scalar_tensor_tensor", dl, dh,
                               -1.0, srcv, ALU.mult, ALU.add)
        barrier()
        es_pt.close()

        es_w = contextlib.ExitStack()
        WQG = sb(es_w, "WQG", [128, 2 * DM], BF16, side="right")
        WQU = sb(es_w, "WQU", [128, 2 * DM], BF16, side="right")
        es_pg = contextlib.ExitStack()
        PG = [ps(es_pg, f"PG{j}", [128, 512]) for j in range(4)]
        PSS = ps(es_pg, "PSS", [1, 512])
        gu_grp = {}
        gu_ld = {}
        gu_ev = [0, 0]
        hsq_done = {}
        ss_mm = {}

        def emit_gu_mms(hp):
            sl = hp % 2
            w = hp % 2
            if hp >= 2:
                wait("sync", sPE, gu_ld[hp - 2])
            dma("sync", WQG[:, sl * DM:(sl + 1) * DM], wqgscr[hp, :, :],
                sem=sS[sl])
            t2 = dma("sync", WQU[:, sl * DM:(sl + 1) * DM],
                     wqgscr[64 + hp, :, :], sem=sS[sl])
            pg, pu = PG[w * 2], PG[w * 2 + 1]
            wait("tensor", sS[sl], t2)
            wait("tensor", sDV, gu_ev[w])
            for c in range(NCH):
                wg = WQG[:, sl * DM + c * 128: sl * DM + (c + 1) * 128]
                wu = WQU[:, sl * DM + c * 128: sl * DM + (c + 1) * 128]
                xh = X1TH[:, c * TBT:(c + 1) * TBT]
                xl = X1TL[:, c * TBT:(c + 1) * TBT]
                op("tensor", "matmul", pg[:], wg, xh,
                   start=(c == 0), stop=False, inc=False)
                op("tensor", "matmul", pg[:], wg, xl,
                   start=False, stop=(c == NCH - 1), inc=False)
                op("tensor", "matmul", pu[:], wu, xh,
                   start=(c == 0), stop=False, inc=False)
                op("tensor", "matmul", pu[:], wu, xl,
                   start=False, stop=(c == NCH - 1), inc=(c == NCH - 1))
            gu_ld[hp] = sPE.n
            gu_grp[hp] = sPE.n

        def emit_gu_evac(hp):
            w = hp % 2
            pg, pu = PG[w * 2], PG[w * 2 + 1]
            wait("scalar", sPE, gu_grp[hp])
            if hp >= 2:
                wait("scalar", sDV, hsq_done[hp - 2])
            ac = op("scalar", "activation", TF0f[:, w * 512:(w + 1) * 512],
                    pg[:], AF.Silu)
            wait("vector", sAC, ac)
            if hp >= 2:
                wait("vector", sPE, ss_mm.get(hp - 2, 0))
            op("vector", "tensor_tensor", TF1f[:],
               TF0f[:, w * 512:(w + 1) * 512], pu[:], ALU.mult, inc=False)
            op("vector", "tensor_tensor", TF2f[:, w * 512:(w + 1) * 512],
               TF1f[:], TF1f[:], ALU.mult, inc=False)
            op("vector", "tensor_copy", HH[:, hp * TBT:(hp + 1) * TBT],
               TF1f[:], inc=False)
            hsq_done[hp] = op("vector", "scalar_tensor_tensor",
                              HL[:, hp * TBT:(hp + 1) * TBT],
                              HH[:, hp * TBT:(hp + 1) * TBT], -1.0, TF1f[:],
                              ALU.mult, ALU.add)
            gu_ev[w] = hsq_done[hp]
            wait("tensor", sDV, hsq_done[hp])
            ss_mm[hp] = op("tensor", "matmul", PSS[:], ONESF[:],
                           TF2f[:, w * 512:(w + 1) * 512],
                           start=(hp == 0), stop=(hp == 63), inc=True)

        for hp in range(64):
            emit_gu_mms(hp)
            if hp >= 1:
                emit_gu_evac(hp - 1)
        emit_gu_evac(63)

        wait("scalar", sPE, ss_mm[63])
        op("scalar", "activation", LNR[:, 0:512], PSS[:], AF.Ln,
           scale=1.0 / FH, bias=RMS_EPS, inc=False)
        ac = op("scalar", "activation", LNR[:, 512:1024], LNR[:, 0:512],
                AF.Exp, scale=-0.5)
        wait("tensor", sAC, ac)
        for g in range(4):
            op("tensor", "transpose", PG[0][:, g:g + 1],
               LNR[0:1, 512 + g * 128: 512 + (g + 1) * 128], IDF[0:1, 0:1],
               inc=(g == 3))
        wait("vector", sPE, sPE.n)
        rst_ev = op("vector", "tensor_copy", RSTB[:], PG[0][:, 0:4])
        barrier()
        es_pg.close()
        es_w.close()
        es_x.close()

        es_w2 = contextlib.ExitStack()
        WQP = sb(es_w2, "WQP", [128, 2 * DM], BF16, side="right")
        X2Q = sb(es_w2, "X2Q", [128, 4 * DM], F32, side="right")
        XR2 = sb(es_w2, "XR2", [128, 2 * 512], F32, side="right")
        YN = sb(es_w2, "YN", [128, 2 * DM], F32, side="right")
        es_pp = contextlib.ExitStack()
        PP = [ps(es_pp, f"PP{j}", [128, 512]) for j in range(8)]
        y_st = {}
        x2_ev = 0
        xr2_used = [0, 0]
        for half in range(2):
            subs = (0, 1) if half == 0 else (2, 3)
            wqp_ld = {}
            for c in range(64):
                sl = c % 2
                if c >= 2:
                    wait("sync", sPE, wqp_ld[c - 2])
                t = dma("sync", WQP[:, sl * DM:(sl + 1) * DM], wqpscr[c, :, :],
                        sem=sS[sl])
                wait("tensor", sS[sl], t)
                if c == 0 and half == 1:
                    wait("tensor", sDV, x2_ev)
                for si, s in enumerate(subs):
                    hh = HH[:, c * TBT + s * 128: c * TBT + (s + 1) * 128]
                    hl = HL[:, c * TBT + s * 128: c * TBT + (s + 1) * 128]
                    for db in range(4):
                        wq = WQP[:, sl * DM + db * 512:
                                 sl * DM + (db + 1) * 512]
                        bank = PP[si * 4 + db]
                        op("tensor", "matmul", bank[:], hh, wq,
                           start=(c == 0), stop=False, inc=False)
                        op("tensor", "matmul", bank[:], hl, wq,
                           start=False, stop=(c == 63),
                           inc=(si == len(subs) - 1 and db == 3))
                wqp_ld[c] = sPE.n
            grp = sPE.n
            for si, s in enumerate(subs):
                for db in range(4):
                    sl2 = db % 2
                    wait("sync", sDV, xr2_used[sl2])
                    t = dma("sync", XR2[:, sl2 * 512:(sl2 + 1) * 512],
                            x1scr[tb * QBT + s * 128: tb * QBT + (s + 1) * 128,
                                  db * 512:(db + 1) * 512], sem=sS[2 + sl2])
                    wait("vector", sS[2 + sl2], t)
                    wait("vector", sPE, grp)
                    wait("vector", sDV, rst_ev)
                    x2_ev = op("vector", "scalar_tensor_tensor",
                               X2Q[:, si * DM + db * 512:
                                   si * DM + (db + 1) * 512],
                               PP[si * 4 + db][:], RSTB[:, s:s + 1],
                               XR2[:, sl2 * 512:(sl2 + 1) * 512],
                               ALU.mult, ALU.add)
                    xr2_used[sl2] = x2_ev
            for si, s in enumerate(subs):
                xq = X2Q[:, si * DM:(si + 1) * DM]
                op("vector", "tensor_reduce", TS4f[:, 0:1], xq.unsqueeze(1),
                   AX.X, ALU.add, inc=False)
                op("vector", "tensor_scalar", TS4f[:, 1:2], TS4f[:, 0:1],
                   1.0 / DM, None, ALU.mult, inc=False)
                for jj in range(4):
                    op("vector", "scalar_tensor_tensor", TF0f[:, 0:512],
                       xq[:, jj * 512:(jj + 1) * 512], 1.0,
                       xq[:, jj * 512:(jj + 1) * 512], ALU.mult, ALU.mult,
                       accum_out=TS4f[:, 2 + jj:3 + jj], inc=False)
                op("vector", "tensor_reduce", TS4f[:, 6:7],
                   TS4f[:, 2:6].unsqueeze(1), AX.X, ALU.add, inc=False)
                op("vector", "tensor_scalar", TS4f[:, 7:8], TS4f[:, 6:7],
                   1.0 / DM, None, ALU.mult, inc=False)
                op("vector", "tensor_tensor", TS4f[:, 8:9], TS4f[:, 1:2],
                   TS4f[:, 1:2], ALU.mult, inc=False)
                dv = op("vector", "tensor_tensor", TS4f[:, 9:10],
                        TS4f[:, 7:8], TS4f[:, 8:9], ALU.subtract)
                wait("scalar", sDV, dv)
                op("scalar", "activation", TS4f[:, 10:11], TS4f[:, 9:10],
                   AF.Ln, bias=LN_EPS, inc=False)
                ac = op("scalar", "activation", TS4f[:, 11:12],
                        TS4f[:, 10:11], AF.Exp, scale=-0.5)
                wait("vector", sAC, ac)
                sly = s % 2
                if s >= 2:
                    wait("vector", sT[sly], y_st[s - 2])
                dv = op("vector", "tensor_scalar",
                        YN[:, sly * DM:(sly + 1) * DM], xq,
                        TS4f[:, 1:2], TS4f[:, 11:12], ALU.subtract, ALU.mult)
                wait("gpsimd", sDV, dv)
                y_st[s] = dma(
                    "gpsimd",
                    y[tb * QBT + s * 128: tb * QBT + (s + 1) * 128, :],
                    YN[:, sly * DM:(sly + 1) * DM], sem=sT[sly])
        barrier()
        es_pp.close()
        es_w2.close()

    wait("gpsimd", sST, sST.n)
    wait("gpsimd", sT[0], sT[0].n)
    wait("gpsimd", sT[1], sT[1].n)
    es_f.close()

    with nc.Block() as block:
        @block.sync
        def _(e):
            for f in prog["sync"]:
                f(e)

        @block.vector
        def _(e):
            for f in prog["vector"]:
                f(e)

        @block.scalar
        def _(e):
            for f in prog["scalar"]:
                f(e)

        @block.tensor
        def _(e):
            for f in prog["tensor"]:
                f(e)

        @block.gpsimd
        def _(e):
            for f in prog["gpsimd"]:
                f(e)

    es_glob.close()
    es_sem.close()
    return nc


def _host_constants():
    try:
        import jax
        import jax.numpy as jnp
        cpus = jax.devices("cpu")
        with jax.default_device(cpus[0]):
            inv = 1.0 / (1000000.0 ** (jnp.arange(0, HD, 2, dtype=jnp.float32)
                                       / HD))
            freqs = jnp.outer(jnp.arange(S, dtype=jnp.float32), inv)
            cos = np.asarray(jnp.cos(freqs))
            sin = np.asarray(jnp.sin(freqs))
    except Exception:
        inv = 1.0 / (1000000.0 ** (np.arange(0, HD, 2, dtype=np.float32)
                                   / np.float32(HD)))
        freqs = np.outer(np.arange(S, dtype=np.float32),
                         inv).astype(np.float32)
        cos = np.cos(freqs).astype(np.float32)
        sin = np.sin(freqs).astype(np.float32)
    c2 = np.empty((S, HD), np.float32)
    c2[:, 0::2] = cos
    c2[:, 1::2] = cos
    s2 = np.empty((S, HD), np.float32)
    s2[:, 0::2] = -sin
    s2[:, 1::2] = sin
    identf = np.eye(128, dtype=np.float32)
    mask = np.zeros((4, 128, QBT), ml_dtypes.bfloat16)
    for j in range(4):
        for p in range(128):
            mask[j, p, 128 * j + p:] = 1.0
    blk = np.zeros((128, 128), ml_dtypes.bfloat16)
    blk[:64, :64] = 1
    blk[64:, 64:] = 1
    return c2, s2, identf, mask, blk


def kernel(L0_attn_x, L0_attn_w_qkv, L0_attn_w_o, L0_attn_q_norm_w,
           L0_attn_q_norm_b, L0_attn_k_norm_w, L0_attn_k_norm_b,
           L0_ffn_w_gate_up, L0_ffn_w_proj):
    x = np.asarray(L0_attn_x, np.float32)
    wqkvgT = np.ascontiguousarray(np.asarray(L0_attn_w_qkv, np.float32).T)
    woT = np.ascontiguousarray(np.asarray(L0_attn_w_o, np.float32).T)
    wguT = np.ascontiguousarray(np.asarray(L0_ffn_w_gate_up, np.float32).T)
    wpT = np.ascontiguousarray(np.asarray(L0_ffn_w_proj, np.float32).T)
    qkn = np.stack([
        np.broadcast_to(np.asarray(L0_attn_q_norm_w, np.float32), (128, HD)),
        np.broadcast_to(np.asarray(L0_attn_q_norm_b, np.float32), (128, HD)),
        np.broadcast_to(np.asarray(L0_attn_k_norm_w, np.float32), (128, HD)),
        np.broadcast_to(np.asarray(L0_attn_k_norm_b, np.float32), (128, HD)),
    ]).copy()
    c2, s2, identf, mask, blk = _host_constants()

    nc = build()
    in_maps = []
    for b in range(8):
        in_maps.append({
            "xT": np.ascontiguousarray(x[b].T),
            "xR": np.ascontiguousarray(x[b]),
            "wqkvgT": wqkvgT, "woT": woT, "wguT": wguT, "wpT": wpT,
            "ropeC2": c2, "ropeS2": s2, "qkn": qkn,
            "identf": identf, "mask01": mask, "blkones": blk,
        })
    r = run_bass_kernel_spmd(nc, in_maps, core_ids=list(range(8)))
    return np.stack([np.asarray(rr["y"]) for rr in r.results], axis=0)


if __name__ == "__main__":
    build()
    print("built ok")


# version 13
# speedup vs baseline: 1.1062x; 1.1062x over previous
import math
import contextlib
import numpy as np
import ml_dtypes
import concourse.bass as bass
import concourse.mybir as mybir
from concourse.bass_utils import run_bass_kernel_spmd

F32 = mybir.dt.float32
BF16 = mybir.dt.bfloat16
AF = mybir.ActivationFunctionType
ALU = mybir.AluOpType
AX = mybir.AxisListType

S = 2048
DM = 2048
NH = 16
NKV = 4
HD = 128
FH = 8192
NQB = 4
QBT = 512
NTB = 4
TBT = 512
NCH = 16

LN_EPS = 1e-5
QK_EPS = 1e-6
RMS_EPS = 1.1920928955078125e-07
RSCALE = 1.0 / math.sqrt(128.0)
M_TIE = 0.501953125

ENGS = ("sync", "vector", "scalar", "tensor", "gpsimd")


def build(dbg=False):
    nc = bass.Bass("TRN2", target_bir_lowering=False, debug=False,
                   detect_race_conditions=False)

    xT = nc.dram_tensor("xT", [DM, S], F32, kind="ExternalInput").ap()
    xR = nc.dram_tensor("xR", [S, DM], F32, kind="ExternalInput").ap()
    wqkvgT = nc.dram_tensor("wqkvgT", [DM, 5120], F32, kind="ExternalInput").ap()
    woT = nc.dram_tensor("woT", [DM, DM], F32, kind="ExternalInput").ap()
    wguT = nc.dram_tensor("wguT", [DM, 2 * FH], F32, kind="ExternalInput").ap()
    wpT = nc.dram_tensor("wpT", [FH, DM], F32, kind="ExternalInput").ap()
    ropeC2 = nc.dram_tensor("ropeC2", [S, HD], F32, kind="ExternalInput").ap()
    ropeS2 = nc.dram_tensor("ropeS2", [S, HD], F32, kind="ExternalInput").ap()
    qkn = nc.dram_tensor("qkn", [4, 128, HD], F32, kind="ExternalInput").ap()
    identf = nc.dram_tensor("identf", [128, 128], F32, kind="ExternalInput").ap()
    mask01 = nc.dram_tensor("mask01", [4, 128, QBT], BF16, kind="ExternalInput").ap()
    blkones = nc.dram_tensor("blkones", [128, 128], BF16, kind="ExternalInput").ap()

    _scrkind = "ExternalOutput" if dbg else "Internal"
    x1scr = nc.dram_tensor("x1scr", [S, DM], F32, kind=_scrkind).ap()
    wkvgH = nc.dram_tensor("wkvgH", [DM, 5120], BF16, kind="Internal").ap()
    wkvgL = nc.dram_tensor("wkvgL", [DM, 5120], BF16, kind="Internal").ap()
    woH = nc.dram_tensor("woH", [DM, DM], BF16, kind="Internal").ap()
    woL = nc.dram_tensor("woL", [DM, DM], BF16, kind="Internal").ap()
    wqgscr = nc.dram_tensor("wqgscr", [128, 128, DM], BF16, kind=_scrkind).ap()
    wqpscr = nc.dram_tensor("wqpscr", [64, 128, DM], BF16, kind=_scrkind).ap()

    y = nc.dram_tensor("y", [S, DM], F32, kind="ExternalOutput").ap()
    if dbg:
        d_k = nc.dram_tensor("d_k", [2, 128, NKV * S], F32, kind="ExternalOutput").ap()
        d_v = nc.dram_tensor("d_v", [2, 128, NKV * S], F32, kind="ExternalOutput").ap()
        d_q = nc.dram_tensor("d_q", [2, 128, NH * QBT], F32, kind="ExternalOutput").ap()
        d_sg = nc.dram_tensor("d_sg", [128, NH * QBT], F32, kind="ExternalOutput").ap()
        d_g = nc.dram_tensor("d_g", [2, 128, NH * QBT], F32, kind="ExternalOutput").ap()

    prog = {e: [] for e in ENGS}

    def em(eng, fn):
        prog[eng].append(fn)

    es_sem = contextlib.ExitStack()

    class Ct:
        def __init__(self, name):
            self.h = es_sem.enter_context(nc.semaphore(name))
            self.n = 0

    sLD = Ct("sLD")
    sST = Ct("sST")
    sPE = Ct("sPE")
    sDV = Ct("sDV")
    sAC = Ct("sAC")
    sS = [Ct(f"sS{i}") for i in range(4)]
    sT = [Ct(f"sT{i}") for i in range(2)]
    sGP = Ct("sGP")
    ALLS = {"sLD": sLD, "sST": sST, "sPE": sPE, "sDV": sDV, "sAC": sAC,
            "sGP": sGP}
    for i, c in enumerate(sS):
        ALLS[f"sS{i}"] = c
    for i, c in enumerate(sT):
        ALLS[f"sT{i}"] = c
    ECT = {"sync": sLD, "gpsimd": sST, "vector": sDV, "scalar": sAC,
           "tensor": sPE}
    OPCT = dict(ECT, gpsimd=sGP)

    def wait(eng, ct, val):
        if val > 0:
            em(eng, lambda e, h=ct.h, v=val: e.wait_ge(h, v))

    def barrier():
        vals = {k: c.n for k, c in ALLS.items()}
        for eng in ENGS:
            for k, c in ALLS.items():
                wait(eng, c, vals[k])

    def dma(eng, out_ap, in_ap, sem=None):
        ct = sem if sem is not None else ECT[eng]
        em(eng, lambda e, o=out_ap, i=in_ap, h=ct.h:
           e.dma_start(out=o, in_=i).then_inc(h, 16))
        ct.n += 16
        return ct.n

    def op(eng, name, *args, inc=True, **kw):
        ct = OPCT[eng]
        need_drain = False
        if eng in ("vector", "scalar") and args:
            try:
                fb = args[0].free_nbytes()
                el4 = fb == 4 * args[0].free_size()
                need_drain = (fb <= 256 or args[0].partition_size() <= 2
                              or (el4 and fb <= 2048))
            except Exception:
                need_drain = False

        def f(e, name=name, args=args, kw=kw, inc=inc, h=ct.h, dr=need_drain):
            instr = getattr(e, name)(*args, **kw)
            if dr:
                e.drain()
            if inc:
                instr.then_inc(h, 1)
        em(eng, f)
        if inc:
            ct.n += 1
        return ct.n

    def pe_inc():
        em("tensor", lambda e, h=sPE.h: e.sem_inc(h, 1))
        sPE.n += 1
        return sPE.n

    es_glob = contextlib.ExitStack()

    _uid = [0]

    def _un(name):
        _uid[0] += 1
        return f"{name}_{_uid[0]}"

    def sb(es, name, shape, dt, side="left"):
        return es.enter_context(nc.sbuf_tensor(_un(name), shape, dt, side=side))

    def ps(es, name, shape, dt=F32):
        return es.enter_context(nc.psum_tensor(_un(name), shape, dt))

    IDF = sb(es_glob, "IDF", [128, 128], F32)
    BLK = sb(es_glob, "BLK", [128, 128], BF16)
    ONESK = sb(es_glob, "ONESK", [128, 1], BF16)
    ONES1 = sb(es_glob, "ONES1", [1, 128], BF16)
    ONESF = sb(es_glob, "ONESF", [128, 1], F32)

    CBIAS = sb(es_glob, "CBIAS", [128, 4], F32)
    nc.const_aps.register(F32, QK_EPS, CBIAS[:, 0:1])
    nc.const_aps.register(F32, LN_EPS, CBIAS[:, 1:2])
    nc.const_aps.register(F32, RMS_EPS, CBIAS[:, 2:3])

    dma("sync", IDF[:], identf)
    c_consts = dma("sync", BLK[:], blkones)
    op("vector", "memset", ONESK[:], 1.0, inc=False)
    op("vector", "memset", ONES1[:], 1.0, inc=False)
    op("vector", "memset", CBIAS[:, 0:1], QK_EPS, inc=False)
    op("vector", "memset", CBIAS[:, 1:2], LN_EPS, inc=False)
    op("vector", "memset", CBIAS[:, 2:3], RMS_EPS, inc=False)
    c_ones = op("vector", "memset", ONESF[:], 1.0)
    wait("scalar", sDV, c_ones)

    es_q = contextlib.ExitStack()
    QW = sb(es_q, "QW", [128, 2 * DM], F32)
    QA = sb(es_q, "QA", [128, 2 * DM], BF16)
    QB = sb(es_q, "QB", [128, DM], BF16)
    QM = sb(es_q, "QM", [128, DM], BF16)
    QSG = sb(es_q, "QSG", [128, 2 * DM], BF16)
    QT_ = sb(es_q, "QT", [128, DM], BF16)
    QWB = sb(es_q, "QWB", [128, DM], BF16)
    QD = sb(es_q, "QD", [128, DM], BF16)
    QH = sb(es_q, "QH", [128, 2 * DM], BF16)
    es_qp = contextlib.ExitStack()
    QP = [ps(es_qp, f"QP{j}", [128, 512]) for j in range(8)]

    qtiles = [("gu", c, fb) for c in range(NCH) for fb in range(8)]
    qtiles += [("p", c, 0) for c in range(64)]
    qtiles += [("kvg", c, fb) for c in range(NCH) for fb in range(3)]
    qtiles += [("wo", c, 0) for c in range(NCH)]

    q_ld, q_ab, q_mm, q_ev, q_st = {}, {}, {}, {}, {}
    q_free = {}
    wait("tensor", sLD, c_consts)
    for i, (kind, c, fb) in enumerate(qtiles):
        sl = i % 2
        if i >= 2:
            ctf, vf = q_free[i - 2]
            wait("sync", ctf, vf)
        if kind == "gu":
            srcw = wguT[c * 128:(c + 1) * 128, fb * DM:(fb + 1) * DM]
            ncol = DM
        elif kind == "p":
            srcw = wpT[c * 128:(c + 1) * 128, 0:DM]
            ncol = DM
        elif kind == "kvg":
            w0 = fb * 1792
            ncol = min(1792, 5120 - w0)
            srcw = wqkvgT[c * 128:(c + 1) * 128, w0:w0 + ncol]
        else:
            srcw = woT[c * 128:(c + 1) * 128, 0:DM]
            ncol = DM
        q_ld[i] = dma("sync", QW[:, sl * DM: sl * DM + ncol], srcw, sem=sS[sl])
        if kind in ("kvg", "wo"):
            wait("vector", sS[sl], q_ld[i])
            if i >= 2:
                wait("vector", sT[sl], q_st[i - 2])
            op("vector", "tensor_copy", QA[:, sl * DM: sl * DM + ncol],
               QW[:, sl * DM: sl * DM + ncol], inc=False)
            q_ev[i] = op("vector", "scalar_tensor_tensor",
                         QSG[:, sl * DM: sl * DM + ncol],
                         QA[:, sl * DM: sl * DM + ncol], -1.0,
                         QW[:, sl * DM: sl * DM + ncol], ALU.mult, ALU.add)
            q_free[i] = (sDV, q_ev[i])
            wait("gpsimd", sDV, q_ev[i])
            if kind == "kvg":
                dh = wkvgH[c * 128:(c + 1) * 128, w0:w0 + ncol]
                dl = wkvgL[c * 128:(c + 1) * 128, w0:w0 + ncol]
            else:
                dh = woH[c * 128:(c + 1) * 128, :]
                dl = woL[c * 128:(c + 1) * 128, :]
            dma("gpsimd", dh, QA[:, sl * DM: sl * DM + ncol], sem=sT[sl])
            q_st[i] = dma("gpsimd", dl, QSG[:, sl * DM: sl * DM + ncol],
                          sem=sT[sl])
            continue
        wait("scalar", sS[sl], q_ld[i])
        if i >= 2:
            wait("scalar", sPE, q_mm.get(i - 2, 0))
            wait("scalar", sDV, q_ev[i - 2])
        op("scalar", "activation", QA[:, sl * DM:(sl + 1) * DM],
           QW[:, sl * DM:(sl + 1) * DM], AF.Abs, inc=False)
        q_ab[i] = op("scalar", "activation", QSG[:, sl * DM:(sl + 1) * DM],
                     QW[:, sl * DM:(sl + 1) * DM], AF.Sign)
        q_free[i] = (sAC, q_ab[i])
        wait("tensor", sAC, q_ab[i])
        if i >= 2:
            wait("tensor", sDV, q_ev[i - 2])
        for j in range(4):
            q_mm[i] = op("tensor", "matmul", QP[(sl * 4) + j][:], BLK[:],
                         QA[:, sl * DM + j * 512: sl * DM + (j + 1) * 512],
                         start=True, stop=True, inc=(j == 3))
        wait("vector", sPE, q_mm[i])
        if i >= 2:
            wait("vector", sT[sl], q_st[i - 2])
        for j in range(4):
            op("vector", "tensor_scalar", QB[:, j * 512:(j + 1) * 512],
               QP[(sl * 4) + j][:], 0.015625, 1e-8, ALU.mult, ALU.max,
               inc=False)
        op("vector", "scalar_tensor_tensor", QM[:], QB[:], M_TIE,
           QA[:, sl * DM:(sl + 1) * DM], ALU.mult, ALU.is_lt, inc=False)
        op("vector", "tensor_tensor", QT_[:], QM[:], QB[:], ALU.mult,
           inc=False)
        op("vector", "tensor_tensor", QD[:], QT_[:],
           QA[:, sl * DM:(sl + 1) * DM], ALU.subtract, inc=False)
        op("vector", "tensor_tensor", QWB[:],
           QA[:, sl * DM:(sl + 1) * DM], QD[:], ALU.add, inc=False)
        q_ev[i] = op("vector", "tensor_tensor", QH[:, sl * DM:(sl + 1) * DM],
                     QWB[:], QSG[:, sl * DM:(sl + 1) * DM], ALU.mult)
        wait("gpsimd", sDV, q_ev[i])
        if kind == "gu":
            dst = wqgscr[fb * 16:(fb + 1) * 16, :, c * 128:(c + 1) * 128]
            dst = dst.rearrange("hp p f -> p hp f")
            srcap = QH[:, sl * DM:(sl + 1) * DM].rearrange(
                "p (hp f) -> p hp f", hp=16)
        else:
            dst = wqpscr[c, :, :]
            srcap = QH[:, sl * DM:(sl + 1) * DM]
        q_st[i] = dma("gpsimd", dst, srcap, sem=sT[sl])

    barrier()
    es_qp.close()
    es_q.close()

    es_a = contextlib.ExitStack()
    KH = sb(es_a, "KH", [128, NKV * S], BF16)
    KL = sb(es_a, "KL", [128, NKV * S], BF16)
    VH = sb(es_a, "VH", [128, NKV * S], BF16)
    VL = sb(es_a, "VL", [128, NKV * S], BF16)
    QKN = sb(es_a, "QKN", [128, 4 * HD], F32)
    MSK = sb(es_a, "MSK", [128, 4 * QBT], BF16)
    PCA = sb(es_a, "PCA", [128, 512], F32)
    TF0 = sb(es_a, "TF0", [128, 512], F32)
    TF1 = sb(es_a, "TF1", [128, 512], F32)
    TF2 = sb(es_a, "TF2", [128, 512], F32)
    TS4 = sb(es_a, "TS4", [128, 64], F32)

    dma("sync", QKN[:].rearrange("p (j f) -> p j f", j=4),
        qkn.rearrange("j p f -> p j f"))
    c_ac = dma("sync", MSK[:].rearrange("p (j f) -> p j f", j=4),
               mask01.rearrange("j p f -> p j f"))
    wait("vector", sLD, c_ac)
    wait("tensor", sLD, c_ac)
    wait("tensor", sDV, c_ones)

    FB_OFF = [0, 512, 1024, 1536, 2048, 2560, 3072, 3584, 4096, 4608]
    FB_KIND = ["q", "q", "q", "q", "g", "g", "g", "g", "k", "v"]

    class PairStream:

        def __init__(self, bh, bl, srch, srcl, sems):
            self.i = 0
            self.rdy = {}
            self.used = {}
            self.bh, self.bl = bh, bl
            self.srch, self.srcl = srch, srcl
            self.sems = sems

        def load(self, r0, c0, c1):
            bh, bl = self.bh, self.bl
            i = self.i
            self.i += 1
            sl = i % 2
            sem = self.sems[sl]
            if i >= 2:
                wait("sync", sPE, self.used[i - 2])
            w = c1 - c0
            srch = self.srch[r0:r0 + 512, c0:c1].rearrange(
                "(c p) f -> p c f", p=128)
            srcl = self.srcl[r0:r0 + 512, c0:c1].rearrange(
                "(c p) f -> p c f", p=128)
            dsth = bh[:, sl * 2048: sl * 2048 + 4 * w].rearrange(
                "p (c f) -> p c f", c=4)
            dstl = bl[:, sl * 2048: sl * 2048 + 4 * w].rearrange(
                "p (c f) -> p c f", c=4)
            dma("sync", dsth, srch, sem=sem)
            self.rdy[i] = dma("sync", dstl, srcl, sem=sem)
            return i, sl

        def chunk(self, sl, cc, w=512):
            return (self.bh[:, sl * 2048 + cc * w: sl * 2048 + (cc + 1) * w],
                    self.bl[:, sl * 2048 + cc * w: sl * 2048 + (cc + 1) * w])

        def consumed(self, i):
            self.used[i] = sPE.n

    for qb in range(NQB):
        es_l = contextlib.ExitStack()
        QTH = sb(es_l, "QTH", [128, NH * QBT], BF16)
        QTL = sb(es_l, "QTL", [128, NH * QBT], BF16)
        SGATE = sb(es_l, "SGATE", [128, NH * QBT], F32)
        es_p1 = contextlib.ExitStack()
        PW = [ps(es_p1, f"PW{j}", [128, 512]) for j in range(8)]
        es_a1 = contextlib.ExitStack()
        WSH = sb(es_a1, "WSH", [128, 2 * 2048], BF16)
        WSL = sb(es_a1, "WSL", [128, 2 * 2048], BF16)
        XT32 = sb(es_a1, "XT32", [128, 2048], F32)
        XTH = sb(es_a1, "XTH", [128, NCH * QBT], BF16)
        XTL = sb(es_a1, "XTL", [128, NCH * QBT], BF16)
        C2 = sb(es_a1, "C2", [128, 4 * HD], F32)
        S2T = sb(es_a1, "S2T", [128, 4 * HD], F32)
        XRP = sb(es_a1, "XRP", [128, 512], F32)

        dma("sync", C2[:].rearrange("p (tt f) -> p tt f", tt=4),
            ropeC2.rearrange("(tt p) f -> p tt f", p=128)[:, qb * 4:qb * 4 + 4, :])
        c_rope = dma("sync", S2T[:].rearrange("p (tt f) -> p tt f", tt=4),
                     ropeS2.rearrange("(tt p) f -> p tt f", p=128)[:, qb * 4:qb * 4 + 4, :])
        wait("vector", sLD, c_rope)

        ws = PairStream(WSH, WSL, wkvgH, wkvgL, (sS[0], sS[1]))
        xt_split = {}
        for cg in range(4):
            if cg >= 1:
                wait("sync", sDV, xt_split[cg - 1])
            srcx = xT[cg * 512:(cg + 1) * 512,
                      qb * QBT:(qb + 1) * QBT].rearrange(
                          "(c p) f -> p c f", p=128)
            dstx = XT32[:].rearrange("p (c f) -> p c f", c=4)
            t = dma("sync", dstx, srcx, sem=sS[2 + cg % 2])
            wait("vector", sS[2 + cg % 2], t)
            op("vector", "tensor_copy",
               XTH[:, cg * 4 * QBT:(cg + 1) * 4 * QBT],
               XT32[:], inc=False)
            xt_split[cg] = op("vector", "scalar_tensor_tensor",
                              XTL[:, cg * 4 * QBT:(cg + 1) * 4 * QBT],
                              XTH[:, cg * 4 * QBT:(cg + 1) * 4 * QBT], -1.0,
                              XT32[:], ALU.mult, ALU.add)
        xt_ready = xt_split[3]
        wave_free = [0, 0]
        grp_done = {}
        pend_evac = []

        def emit_mms(fb):
            kind = FB_KIND[fb]
            w = fb % 2
            banks = PW[w * 4:(w + 1) * 4]
            for c in range(NCH):
                cc = c % 4
                if cc == 0:
                    i, sl = ws.load(c * 128, FB_OFF[fb], FB_OFF[fb] + 512)
                    wait("tensor", ws.sems[sl], ws.rdy[i])
                if c == 0:
                    wait("tensor", sDV, wave_free[w])
                    wait("tensor", sDV, xt_ready)
                wsh, wsl = ws.chunk(sl, cc)
                xth_f = XTH[:, c * QBT:(c + 1) * QBT]
                xtl_f = XTL[:, c * QBT:(c + 1) * QBT]
                if kind == "g":
                    for fs in range(4):
                        op("tensor", "matmul", banks[fs][:],
                           wsh[:, fs * 128:(fs + 1) * 128], xth_f,
                           start=(c == 0), stop=False, inc=False)
                        op("tensor", "matmul", banks[fs][:],
                           wsh[:, fs * 128:(fs + 1) * 128], xtl_f,
                           start=False, stop=False, inc=False)
                        op("tensor", "matmul", banks[fs][:],
                           wsl[:, fs * 128:(fs + 1) * 128], xth_f,
                           start=False, stop=(c == NCH - 1), inc=(fs == 3))
                else:
                    for s in range(4):
                        xth = xth_f[:, s * 128:(s + 1) * 128]
                        xtl = xtl_f[:, s * 128:(s + 1) * 128]
                        op("tensor", "matmul", banks[s][:], xth, wsh,
                           start=(c == 0), stop=False, inc=False)
                        op("tensor", "matmul", banks[s][:], xth, wsl,
                           start=False, stop=False, inc=False)
                        op("tensor", "matmul", banks[s][:], xtl, wsh,
                           start=False, stop=(c == NCH - 1), inc=(s == 3))
                if cc == 3:
                    ws.consumed(i)
            grp_done[fb] = sPE.n

        def emit_evac(fb):
            kind = FB_KIND[fb]
            w = fb % 2
            banks = PW[w * 4:(w + 1) * 4]
            gd = grp_done[fb]
            last = 0
            if kind == "g":
                sgx_read = getattr(emit_evac, "_sgx_read", [0, 0])
                for fs in range(4):
                    hh = (fb - 4) * 4 + fs
                    wait("scalar", sPE, gd)
                    wait("scalar", sDV,
                         max(sgx_read[0], getattr(emit_evac, "_sqg", 0)))
                    ac = op("scalar", "activation",
                            TF2[:], banks[fs][:], AF.Exp, scale=-1.0)
                    wait("vector", sAC, ac)
                    sgx_read[0] = op("vector", "tensor_scalar", TF0[:],
                                     TF2[:], 1.0, None, ALU.add)
                    last = op("vector", "reciprocal",
                              SGATE[:, hh * QBT:(hh + 1) * QBT], TF0[:])
                emit_evac._sgx_read = sgx_read
            elif kind == "v":
                vh = VH[:].rearrange("p (kv tt f) -> p kv tt f", kv=NKV, tt=16)
                vl = VL[:].rearrange("p (kv tt f) -> p kv tt f", kv=NKV, tt=16)
                for s in range(4):
                    tt = qb * 4 + s
                    wait("vector", sPE, gd)
                    op("vector", "tensor_copy", PCA[:], banks[s][:],
                       inc=False)
                    srcv = PCA[:].rearrange("p (kv f) -> p kv f", kv=NKV)
                    op("vector", "tensor_copy", vh[:, :, tt, :], srcv,
                       inc=False)
                    last = op("vector", "scalar_tensor_tensor",
                              vl[:, :, tt, :], vh[:, :, tt, :], -1.0, srcv,
                              ALU.mult, ALU.add)
            else:
                goff = 0 if kind == "q" else 2
                gam = QKN[:, goff * HD:(goff + 1) * HD]
                bet = QKN[:, (goff + 1) * HD:(goff + 2) * HD]
                for s in range(4):
                    tt = qb * 4 + s
                    bank = banks[s]
                    xsl = s % 2
                    sq_guard = max(getattr(emit_evac, "_sqg", 0),
                                   getattr(emit_evac, "_sgx_read", [0, 0])[0])
                    wait("vector", sPE, gd)
                    pcd = op("vector", "tensor_copy", PCA[:], bank[:])
                    wait("scalar", sDV, pcd)
                    wait("scalar", sDV, sq_guard)
                    acq = op("scalar", "activation", TF2[:], PCA[:],
                             AF.Square)
                    wait("vector", sAC, acq)
                    op("vector", "tensor_reduce", TS4[:, 0:4],
                       PCA[:].rearrange("p (g n) -> p g n", g=4), AX.X,
                       ALU.add, inc=False)
                    op("vector", "tensor_scalar", TS4[:, 4:8], TS4[:, 0:4],
                       1.0 / 128.0, None, ALU.mult, inc=False)
                    op("vector", "tensor_reduce", TS4[:, 8:12],
                       TF2[:].rearrange("p (g n) -> p g n", g=4), AX.X,
                       ALU.add, inc=False)
                    op("vector", "tensor_scalar", TS4[:, 12:16], TS4[:, 8:12],
                       1.0 / 128.0, None, ALU.mult, inc=False)
                    op("vector", "tensor_tensor", TS4[:, 16:20], TS4[:, 4:8],
                       TS4[:, 4:8], ALU.mult, inc=False)
                    dv = op("vector", "tensor_tensor", TS4[:, 20:24],
                            TS4[:, 12:16], TS4[:, 16:20], ALU.subtract)
                    wait("scalar", sDV, dv)
                    op("scalar", "activation", TS4[:, 24:28], TS4[:, 20:24],
                       AF.Ln, bias=QK_EPS, inc=False)
                    ac = op("scalar", "activation", TS4[:, 28:32],
                            TS4[:, 24:28], AF.Exp, scale=-0.5)
                    wait("vector", sAC, ac)
                    pcv = PCA[:].rearrange("p (g f) -> p g f", g=4)
                    op("vector", "tensor_tensor",
                       TF1[:].rearrange("p (g f) -> p g f", g=4), pcv,
                       TS4[:, 4:8].unsqueeze(2).broadcast_to((128, 4, HD)),
                       ALU.subtract, inc=False)
                    op("vector", "tensor_tensor",
                       TF1[:].rearrange("p (g f) -> p g f", g=4),
                       TF1[:].rearrange("p (g f) -> p g f", g=4),
                       TS4[:, 28:32].unsqueeze(2).broadcast_to((128, 4, HD)),
                       ALU.mult, inc=False)
                    t1v = TF1[:].rearrange("p (g f) -> p g f", g=4)
                    t2v = TF2[:].rearrange("p (g f) -> p g f", g=4)
                    op("vector", "tensor_tensor", t2v, t1v,
                       gam.unsqueeze(1).broadcast_to((128, 4, HD)), ALU.mult,
                       inc=False)
                    op("vector", "tensor_tensor", t1v, t2v,
                       bet.unsqueeze(1).broadcast_to((128, 4, HD)), ALU.add,
                       inc=False)
                    if True:
                        pass
                    xrp_tr = getattr(emit_evac, "_xrp_tr", [0, 0])
                    wait("vector", sPE, xrp_tr[0])
                    xr = XRP[:]
                    c2b = C2[:, s * HD:(s + 1) * HD].unsqueeze(1)
                    c2b = c2b.broadcast_to((128, 4, HD))
                    s2b = S2T[:, s * HD:(s + 1) * HD].rearrange(
                        "p (pr two) -> p pr two", two=2).unsqueeze(1)
                    s2b = s2b.broadcast_to((128, 4, 64, 2))
                    op("vector", "tensor_tensor", TF0[:].rearrange(
                        "p (g f) -> p g f", g=4), t1v, c2b, ALU.mult,
                       inc=False)
                    swap = TF1[:].rearrange("p (g pr two) -> p g pr two",
                                            g=4, two=2)[:, :, :, ::-1]
                    op("vector", "tensor_tensor", TF2[:].rearrange(
                        "p (g pr two) -> p g pr two", g=4, two=2), swap, s2b,
                       ALU.mult, inc=False)
                    rp = op("vector", "tensor_tensor", xr, TF0[:], TF2[:],
                            ALU.add)
                    emit_evac._sqg = rp
                    wait("tensor", sDV, rp)
                    for g in range(4):
                        op("tensor", "transpose",
                           bank[:, g * 128:(g + 1) * 128],
                           xr[:, g * 128:(g + 1) * 128], IDF[:],
                           inc=(g == 3))
                    trd = sPE.n
                    xrp_tr[0] = trd
                    emit_evac._xrp_tr = xrp_tr
                    wait("vector", sPE, trd)
                    if kind == "q":
                        dsth = QTH[:].rearrange("p (hh t) -> p hh t", hh=NH)
                        dstl = QTL[:].rearrange("p (hh t) -> p hh t", hh=NH)
                        dh = dsth[:, fb * 4:fb * 4 + 4, s * 128:(s + 1) * 128]
                        dl = dstl[:, fb * 4:fb * 4 + 4, s * 128:(s + 1) * 128]
                    else:
                        dsth = KH[:].rearrange("p (kv t) -> p kv t", kv=NKV)
                        dstl = KL[:].rearrange("p (kv t) -> p kv t", kv=NKV)
                        dh = dsth[:, :, qb * QBT + s * 128:
                                  qb * QBT + (s + 1) * 128]
                        dl = dstl[:, :, qb * QBT + s * 128:
                                  qb * QBT + (s + 1) * 128]
                    op("vector", "tensor_copy", PCA[:], bank[:], inc=False)
                    srcv = PCA[:].rearrange("p (g f) -> p g f", g=4)
                    op("vector", "tensor_copy", dh, srcv, inc=False)
                    last = op("vector", "scalar_tensor_tensor", dl, dh, -1.0,
                              srcv, ALU.mult, ALU.add)
            wave_free[w] = last

        for fb in range(10):
            emit_mms(fb)
            if fb >= 1:
                emit_evac(fb - 1)
        emit_evac(9)

        barrier()
        if dbg and qb == 0:
            dma("gpsimd", d_k[0], KH[:])
            dma("gpsimd", d_k[1], KL[:])
            dma("gpsimd", d_v[0], VH[:])
            dma("gpsimd", d_v[1], VL[:])
            dma("gpsimd", d_q[0], QTH[:])
            dma("gpsimd", d_q[1], QTL[:])
            dma("gpsimd", d_sg, SGATE[:])
            wait("gpsimd", sST, sST.n)
            barrier()
        es_p1.close()
        es_a1.close()

        es_r2 = contextlib.ExitStack()
        GH = sb(es_r2, "GH", [128, NH * QBT], BF16, side="right")
        GL = sb(es_r2, "GL", [128, NH * QBT], BF16, side="right")
        EF = sb(es_r2, "EF", [128, 2 * 512], F32, side="right")
        EHB = sb(es_r2, "EHB", [128, 2 * 512], BF16, side="right")
        ELB = sb(es_r2, "ELB", [128, 2 * 512], BF16, side="right")
        RD = sb(es_r2, "RD", [1, 512], F32, side="right")
        RDH = sb(es_r2, "RDH", [1, 512], BF16, side="right")
        RDL = sb(es_r2, "RDL", [1, 512], BF16, side="right")
        RB = sb(es_r2, "RB", [128, 512], F32, side="right")
        es_p2 = contextlib.ExitStack()
        PS_ = [ps(es_p2, "PSa", [128, 512]), ps(es_p2, "PSb", [128, 512])]
        PAV = ps(es_p2, "PAV", [128, 512])
        PDN = ps(es_p2, "PDN", [1, 512])
        PRB = ps(es_p2, "PRB", [128, 512])

        nk = 4 * (qb + 1)
        gate_done = 0
        for h in range(NH):
            kvh = h // 4
            qh_ap = QTH[:, h * QBT:(h + 1) * QBT]
            ql_ap = QTL[:, h * QBT:(h + 1) * QBT]
            exp_done, el_done, av_done = {}, {}, {}

            def emit_S(kc):
                sbk = kc % 2
                if kc >= 2:
                    wait("tensor", sAC, exp_done[kc - 2])
                kh = KH[:, kvh * S + kc * 128: kvh * S + (kc + 1) * 128]
                kl = KL[:, kvh * S + kc * 128: kvh * S + (kc + 1) * 128]
                op("tensor", "matmul", PS_[sbk][:], kh, qh_ap,
                   start=True, stop=False, inc=False)
                op("tensor", "matmul", PS_[sbk][:], kh, ql_ap,
                   start=False, stop=False, inc=False)
                op("tensor", "matmul", PS_[sbk][:], kl, qh_ap,
                   start=False, stop=True, inc=True)
                wait("scalar", sPE, sPE.n)
                if kc >= 2:
                    wait("scalar", sDV, el_done[kc - 2])
                exp_done[kc] = op("scalar", "activation",
                                  EF[:, sbk * 512:(sbk + 1) * 512],
                                  PS_[sbk][:], AF.Exp, scale=RSCALE)
                wait("vector", sAC, exp_done[kc])
                if kc >= 2:
                    wait("vector", sPE, av_done[kc - 2])
                efs = EF[:, sbk * 512:(sbk + 1) * 512]
                if kc >= 4 * qb:
                    j = kc - 4 * qb
                    op("vector", "tensor_tensor", efs, efs,
                       MSK[:, j * QBT:(j + 1) * QBT], ALU.mult, inc=False)
                op("vector", "tensor_copy",
                   EHB[:, sbk * 512:(sbk + 1) * 512], efs, inc=False)
                el_done[kc] = op("vector", "scalar_tensor_tensor",
                                 ELB[:, sbk * 512:(sbk + 1) * 512],
                                 EHB[:, sbk * 512:(sbk + 1) * 512], -1.0, efs,
                                 ALU.mult, ALU.add)

            def emit_AV(kc):
                sbk = kc % 2
                wait("tensor", sDV, el_done[kc])
                if kc == 0 and gate_done:
                    wait("tensor", sDV, gate_done)
                vh = VH[:, (kvh * 16 + kc) * 128:(kvh * 16 + kc + 1) * 128]
                vl = VL[:, (kvh * 16 + kc) * 128:(kvh * 16 + kc + 1) * 128]
                ehs = EHB[:, sbk * 512:(sbk + 1) * 512]
                els = ELB[:, sbk * 512:(sbk + 1) * 512]
                op("tensor", "matmul", PAV[:], vh, ehs,
                   start=(kc == 0), stop=False, inc=False)
                op("tensor", "matmul", PAV[:], vh, els,
                   start=False, stop=False, inc=False)
                op("tensor", "matmul", PAV[:], vl, ehs,
                   start=False, stop=(kc == nk - 1), inc=False)
                op("tensor", "matmul", PDN[:], ONESK[:], ehs,
                   start=(kc == 0), stop=False, inc=False)
                op("tensor", "matmul", PDN[:], ONESK[:], els,
                   start=False, stop=(kc == nk - 1), inc=True)
                av_done[kc] = sPE.n

            emit_S(0)
            for kc in range(nk):
                if kc + 1 < nk:
                    emit_S(kc + 1)
                emit_AV(kc)
            wait("vector", sPE, av_done[nk - 1])
            op("vector", "reciprocal", RD[:], PDN[:], inc=False)
            op("vector", "tensor_copy", RDH[:], RD[:], inc=False)
            dvv = op("vector", "scalar_tensor_tensor", RDL[:], RDH[:], -1.0,
                     RD[:], ALU.mult, ALU.add)
            wait("tensor", sDV, dvv)
            op("tensor", "matmul", PRB[:], ONES1[:], RDH[:],
               start=True, stop=False, inc=False)
            op("tensor", "matmul", PRB[:], ONES1[:], RDL[:],
               start=False, stop=True, inc=True)
            wait("vector", sPE, sPE.n)
            op("vector", "tensor_copy", RB[:], PRB[:], inc=False)
            op("vector", "tensor_tensor", TF0[:], PAV[:], RB[:], ALU.mult,
               inc=False)
            op("vector", "tensor_tensor", TF1[:], TF0[:],
               SGATE[:, h * QBT:(h + 1) * QBT], ALU.mult, inc=False)
            op("vector", "tensor_copy", GH[:, h * QBT:(h + 1) * QBT], TF1[:],
               inc=False)
            gate_done = op("vector", "scalar_tensor_tensor",
                           GL[:, h * QBT:(h + 1) * QBT],
                           GH[:, h * QBT:(h + 1) * QBT], -1.0, TF1[:],
                           ALU.mult, ALU.add)

        barrier()
        if dbg and qb == 0:
            dma("gpsimd", d_g[0], GH[:])
            dma("gpsimd", d_g[1], GL[:])
            wait("gpsimd", sST, sST.n)
            barrier()
        es_p2.close()
        es_l.close()

        es_l2 = contextlib.ExitStack()
        X1Q = sb(es_l2, "X1Q", [128, 4 * DM], F32)
        WSH = sb(es_l2, "WSHb", [128, 2 * 2048], BF16)
        WSL = sb(es_l2, "WSLb", [128, 2 * 2048], BF16)
        XRES = sb(es_l2, "XRES", [128, 2 * 512], F32)
        X1N = sb(es_l2, "X1N", [128, DM], F32)
        es_p3 = contextlib.ExitStack()
        PWO = [ps(es_p3, f"PWO{j}", [128, 512]) for j in range(8)]
        ws = PairStream(WSH, WSL, woH, woL, (sS[0], sS[1]))
        wo_wave_free = [0, 0]
        wo_grp = {}
        xres_used = [0, 0]

        def emit_wo_mms(db):
            w = db % 2
            banks = PWO[w * 4:(w + 1) * 4]
            for fc in range(NCH):
                cc = fc % 4
                if cc == 0:
                    i, sl = ws.load(fc * 128, db * 512, (db + 1) * 512)
                    wait("tensor", ws.sems[sl], ws.rdy[i])
                if fc == 0:
                    wait("tensor", sDV, wo_wave_free[w])
                wsh, wsl = ws.chunk(sl, cc)
                for s in range(4):
                    gh = GH[:, fc * QBT + s * 128: fc * QBT + (s + 1) * 128]
                    gl = GL[:, fc * QBT + s * 128: fc * QBT + (s + 1) * 128]
                    op("tensor", "matmul", banks[s][:], gh, wsh,
                       start=(fc == 0), stop=False, inc=False)
                    op("tensor", "matmul", banks[s][:], gh, wsl,
                       start=False, stop=False, inc=False)
                    op("tensor", "matmul", banks[s][:], gl, wsh,
                       start=False, stop=(fc == NCH - 1), inc=(s == 3))
                if cc == 3:
                    ws.consumed(i)
            wo_grp[db] = sPE.n

        def emit_wo_evac(db):
            nonlocal wo_wave_free, xres_used
            w = db % 2
            banks = PWO[w * 4:(w + 1) * 4]
            for s in range(4):
                sl = (db * 4 + s) % 2
                wait("sync", sDV, xres_used[sl])
                t = dma("sync", XRES[:, sl * 512:(sl + 1) * 512],
                        xR[qb * QBT + s * 128: qb * QBT + (s + 1) * 128,
                           db * 512:(db + 1) * 512], sem=sS[2 + sl])
                wait("vector", sS[2 + sl], t)
                wait("vector", sPE, wo_grp[db])
                ev = op("vector", "tensor_tensor",
                        X1Q[:, s * DM + db * 512: s * DM + (db + 1) * 512],
                        banks[s][:], XRES[:, sl * 512:(sl + 1) * 512],
                        ALU.add)
                xres_used[sl] = ev
            wo_wave_free[w] = ev

        for db in range(4):
            emit_wo_mms(db)
            if db >= 1:
                emit_wo_evac(db - 1)
        emit_wo_evac(3)

        x1_st = {}
        for s in range(4):
            xq = X1Q[:, s * DM:(s + 1) * DM]
            op("vector", "tensor_reduce", TS4[:, 0:1], xq.unsqueeze(1),
               AX.X, ALU.add, inc=False)
            op("vector", "tensor_scalar", TS4[:, 1:2], TS4[:, 0:1],
               1.0 / DM, None, ALU.mult, inc=False)
            for jj in range(4):
                op("vector", "scalar_tensor_tensor", TF0[:],
                   xq[:, jj * 512:(jj + 1) * 512], 1.0,
                   xq[:, jj * 512:(jj + 1) * 512], ALU.mult, ALU.mult,
                   accum_out=TS4[:, 2 + jj:3 + jj], inc=False)
            op("vector", "tensor_reduce", TS4[:, 6:7],
               TS4[:, 2:6].unsqueeze(1), AX.X, ALU.add, inc=False)
            op("vector", "tensor_scalar", TS4[:, 7:8], TS4[:, 6:7],
               1.0 / DM, None, ALU.mult, inc=False)
            op("vector", "tensor_tensor", TS4[:, 8:9], TS4[:, 1:2],
               TS4[:, 1:2], ALU.mult, inc=False)
            dv = op("vector", "tensor_tensor", TS4[:, 9:10], TS4[:, 7:8],
                    TS4[:, 8:9], ALU.subtract)
            wait("scalar", sDV, dv)
            op("scalar", "activation", TS4[:, 10:11], TS4[:, 9:10],
               AF.Ln, bias=LN_EPS, inc=False)
            ac = op("scalar", "activation", TS4[:, 11:12], TS4[:, 10:11],
                    AF.Exp, scale=-0.5)
            wait("vector", sAC, ac)
            if s >= 1:
                wait("vector", sST, x1_st[s - 1])
            dv = op("vector", "tensor_scalar", X1N[:],
                    xq, TS4[:, 1:2], TS4[:, 11:12], ALU.subtract, ALU.mult)
            wait("gpsimd", sDV, dv)
            x1_st[s] = dma(
                "gpsimd",
                x1scr[qb * QBT + s * 128: qb * QBT + (s + 1) * 128, :],
                X1N[:])
        barrier()
        es_p3.close()
        es_l2.close()
        es_r2.close()

    barrier()
    es_a.close()

    es_f = contextlib.ExitStack()
    TF0f = sb(es_f, "TF0f", [128, 2 * 512], F32)
    TF1f = sb(es_f, "TF1f", [128, 512], F32)
    TF2f = sb(es_f, "TF2f", [128, 2 * 512], F32)
    TS4f = sb(es_f, "TS4f", [128, 64], F32)
    RSTB = sb(es_f, "RSTB", [128, 4], F32)
    LNR = sb(es_f, "LNR", [1, 2 * 512], F32)
    HH = sb(es_f, "HH", [128, 64 * TBT], BF16)
    HL = sb(es_f, "HL", [128, 64 * TBT], BF16)

    for tb in range(NTB):
        es_x = contextlib.ExitStack()
        X1TH = sb(es_x, "X1TH", [128, NCH * TBT], BF16, side="right")
        X1TL = sb(es_x, "X1TL", [128, NCH * TBT], BF16, side="right")
        X1LD = sb(es_x, "X1LD", [128, 2 * DM], F32, side="right")
        es_pt = contextlib.ExitStack()
        PTR = [ps(es_pt, f"PTR{j}", [128, 512]) for j in range(2)]
        tr_ev = [0, 0]
        s_trdone = {}
        for s in range(4):
            sl = s % 2
            if s >= 2:
                wait("sync", sPE, s_trdone[s - 2])
            t = dma("sync", X1LD[:, sl * DM:(sl + 1) * DM],
                    x1scr[tb * QBT + s * 128: tb * QBT + (s + 1) * 128, :],
                    sem=sS[sl])
            for grp in range(4):
                bk = grp % 2
                bank = PTR[bk]
                wait("tensor", sS[sl], t)
                wait("tensor", sDV, tr_ev[bk])
                for g in range(4):
                    c = grp * 4 + g
                    op("tensor", "transpose", bank[:, g * 128:(g + 1) * 128],
                       X1LD[:, sl * DM + c * 128: sl * DM + (c + 1) * 128],
                       IDF[:], inc=(g == 3))
                s_trdone[s] = sPE.n
                wait("vector", sPE, sPE.n)
                dsth = X1TH[:].rearrange("p (c t) -> p c t", c=NCH)
                dstl = X1TL[:].rearrange("p (c t) -> p c t", c=NCH)
                dh = dsth[:, grp * 4:(grp + 1) * 4, s * 128:(s + 1) * 128]
                dl = dstl[:, grp * 4:(grp + 1) * 4, s * 128:(s + 1) * 128]
                op("vector", "tensor_copy", TF0f[:, 0:512], bank[:],
                   inc=False)
                srcv = TF0f[:, 0:512].rearrange("p (g f) -> p g f", g=4)
                op("vector", "tensor_copy", dh, srcv, inc=False)
                tr_ev[bk] = op("vector", "scalar_tensor_tensor", dl, dh,
                               -1.0, srcv, ALU.mult, ALU.add)
        barrier()
        es_pt.close()

        es_w = contextlib.ExitStack()
        WQG = sb(es_w, "WQG", [128, 2 * DM], BF16, side="right")
        WQU = sb(es_w, "WQU", [128, 2 * DM], BF16, side="right")
        es_pg = contextlib.ExitStack()
        PG = [ps(es_pg, f"PG{j}", [128, 512]) for j in range(4)]
        PSS = ps(es_pg, "PSS", [1, 512])
        gu_grp = {}
        gu_ld = {}
        gu_ev = [0, 0]
        hsq_done = {}
        ss_mm = {}

        def emit_gu_mms(hp):
            sl = hp % 2
            w = hp % 2
            if hp >= 2:
                wait("sync", sPE, gu_ld[hp - 2])
            dma("sync", WQG[:, sl * DM:(sl + 1) * DM], wqgscr[hp, :, :],
                sem=sS[sl])
            t2 = dma("sync", WQU[:, sl * DM:(sl + 1) * DM],
                     wqgscr[64 + hp, :, :], sem=sS[sl])
            pg, pu = PG[w * 2], PG[w * 2 + 1]
            wait("tensor", sS[sl], t2)
            wait("tensor", sDV, gu_ev[w])
            for c in range(NCH):
                wg = WQG[:, sl * DM + c * 128: sl * DM + (c + 1) * 128]
                wu = WQU[:, sl * DM + c * 128: sl * DM + (c + 1) * 128]
                xh = X1TH[:, c * TBT:(c + 1) * TBT]
                xl = X1TL[:, c * TBT:(c + 1) * TBT]
                op("tensor", "matmul", pg[:], wg, xh,
                   start=(c == 0), stop=False, inc=False)
                op("tensor", "matmul", pg[:], wg, xl,
                   start=False, stop=(c == NCH - 1), inc=False)
                op("tensor", "matmul", pu[:], wu, xh,
                   start=(c == 0), stop=False, inc=False)
                op("tensor", "matmul", pu[:], wu, xl,
                   start=False, stop=(c == NCH - 1), inc=(c == NCH - 1))
            gu_ld[hp] = sPE.n
            gu_grp[hp] = sPE.n

        def emit_gu_evac(hp):
            w = hp % 2
            pg, pu = PG[w * 2], PG[w * 2 + 1]
            wait("scalar", sPE, gu_grp[hp])
            if hp >= 2:
                wait("scalar", sDV, hsq_done[hp - 2])
            ac = op("scalar", "activation", TF0f[:, w * 512:(w + 1) * 512],
                    pg[:], AF.Silu)
            wait("vector", sAC, ac)
            if hp >= 2:
                wait("vector", sPE, ss_mm.get(hp - 2, 0))
            op("vector", "tensor_tensor", TF1f[:],
               TF0f[:, w * 512:(w + 1) * 512], pu[:], ALU.mult, inc=False)
            op("vector", "tensor_tensor", TF2f[:, w * 512:(w + 1) * 512],
               TF1f[:], TF1f[:], ALU.mult, inc=False)
            op("vector", "tensor_copy", HH[:, hp * TBT:(hp + 1) * TBT],
               TF1f[:], inc=False)
            hsq_done[hp] = op("vector", "scalar_tensor_tensor",
                              HL[:, hp * TBT:(hp + 1) * TBT],
                              HH[:, hp * TBT:(hp + 1) * TBT], -1.0, TF1f[:],
                              ALU.mult, ALU.add)
            gu_ev[w] = hsq_done[hp]
            wait("tensor", sDV, hsq_done[hp])
            ss_mm[hp] = op("tensor", "matmul", PSS[:], ONESF[:],
                           TF2f[:, w * 512:(w + 1) * 512],
                           start=(hp == 0), stop=(hp == 63), inc=True)

        for hp in range(64):
            emit_gu_mms(hp)
            if hp >= 1:
                emit_gu_evac(hp - 1)
        emit_gu_evac(63)

        wait("scalar", sPE, ss_mm[63])
        op("scalar", "activation", LNR[:, 0:512], PSS[:], AF.Ln,
           scale=1.0 / FH, bias=RMS_EPS, inc=False)
        ac = op("scalar", "activation", LNR[:, 512:1024], LNR[:, 0:512],
                AF.Exp, scale=-0.5)
        wait("tensor", sAC, ac)
        for g in range(4):
            op("tensor", "transpose", PG[0][:, g:g + 1],
               LNR[0:1, 512 + g * 128: 512 + (g + 1) * 128], IDF[0:1, 0:1],
               inc=(g == 3))
        wait("vector", sPE, sPE.n)
        rst_ev = op("vector", "tensor_copy", RSTB[:], PG[0][:, 0:4])
        barrier()
        es_pg.close()
        es_w.close()
        es_x.close()

        es_w2 = contextlib.ExitStack()
        WQP = sb(es_w2, "WQP", [128, 2 * DM], BF16, side="right")
        X2Q = sb(es_w2, "X2Q", [128, 4 * DM], F32, side="right")
        XR2 = sb(es_w2, "XR2", [128, 2 * 512], F32, side="right")
        YN = sb(es_w2, "YN", [128, 2 * DM], F32, side="right")
        es_pp = contextlib.ExitStack()
        PP = [ps(es_pp, f"PP{j}", [128, 512]) for j in range(8)]
        y_st = {}
        x2_ev = 0
        xr2_used = [0, 0]
        for half in range(2):
            subs = (0, 1) if half == 0 else (2, 3)
            wqp_ld = {}
            for c in range(64):
                sl = c % 2
                if c >= 2:
                    wait("sync", sPE, wqp_ld[c - 2])
                t = dma("sync", WQP[:, sl * DM:(sl + 1) * DM], wqpscr[c, :, :],
                        sem=sS[sl])
                wait("tensor", sS[sl], t)
                if c == 0 and half == 1:
                    wait("tensor", sDV, x2_ev)
                for si, s in enumerate(subs):
                    hh = HH[:, c * TBT + s * 128: c * TBT + (s + 1) * 128]
                    hl = HL[:, c * TBT + s * 128: c * TBT + (s + 1) * 128]
                    for db in range(4):
                        wq = WQP[:, sl * DM + db * 512:
                                 sl * DM + (db + 1) * 512]
                        bank = PP[si * 4 + db]
                        op("tensor", "matmul", bank[:], hh, wq,
                           start=(c == 0), stop=False, inc=False)
                        op("tensor", "matmul", bank[:], hl, wq,
                           start=False, stop=(c == 63),
                           inc=(si == len(subs) - 1 and db == 3))
                wqp_ld[c] = sPE.n
            grp = sPE.n
            for si, s in enumerate(subs):
                for db in range(4):
                    sl2 = db % 2
                    wait("sync", sDV, xr2_used[sl2])
                    t = dma("sync", XR2[:, sl2 * 512:(sl2 + 1) * 512],
                            x1scr[tb * QBT + s * 128: tb * QBT + (s + 1) * 128,
                                  db * 512:(db + 1) * 512], sem=sS[2 + sl2])
                    wait("vector", sS[2 + sl2], t)
                    wait("vector", sPE, grp)
                    wait("vector", sDV, rst_ev)
                    x2_ev = op("vector", "scalar_tensor_tensor",
                               X2Q[:, si * DM + db * 512:
                                   si * DM + (db + 1) * 512],
                               PP[si * 4 + db][:], RSTB[:, s:s + 1],
                               XR2[:, sl2 * 512:(sl2 + 1) * 512],
                               ALU.mult, ALU.add)
                    xr2_used[sl2] = x2_ev
            for si, s in enumerate(subs):
                xq = X2Q[:, si * DM:(si + 1) * DM]
                op("vector", "tensor_reduce", TS4f[:, 0:1], xq.unsqueeze(1),
                   AX.X, ALU.add, inc=False)
                op("vector", "tensor_scalar", TS4f[:, 1:2], TS4f[:, 0:1],
                   1.0 / DM, None, ALU.mult, inc=False)
                for jj in range(4):
                    op("vector", "scalar_tensor_tensor", TF0f[:, 0:512],
                       xq[:, jj * 512:(jj + 1) * 512], 1.0,
                       xq[:, jj * 512:(jj + 1) * 512], ALU.mult, ALU.mult,
                       accum_out=TS4f[:, 2 + jj:3 + jj], inc=False)
                op("vector", "tensor_reduce", TS4f[:, 6:7],
                   TS4f[:, 2:6].unsqueeze(1), AX.X, ALU.add, inc=False)
                op("vector", "tensor_scalar", TS4f[:, 7:8], TS4f[:, 6:7],
                   1.0 / DM, None, ALU.mult, inc=False)
                op("vector", "tensor_tensor", TS4f[:, 8:9], TS4f[:, 1:2],
                   TS4f[:, 1:2], ALU.mult, inc=False)
                dv = op("vector", "tensor_tensor", TS4f[:, 9:10],
                        TS4f[:, 7:8], TS4f[:, 8:9], ALU.subtract)
                wait("scalar", sDV, dv)
                op("scalar", "activation", TS4f[:, 10:11], TS4f[:, 9:10],
                   AF.Ln, bias=LN_EPS, inc=False)
                ac = op("scalar", "activation", TS4f[:, 11:12],
                        TS4f[:, 10:11], AF.Exp, scale=-0.5)
                wait("vector", sAC, ac)
                sly = s % 2
                if s >= 2:
                    wait("vector", sT[sly], y_st[s - 2])
                dv = op("vector", "tensor_scalar",
                        YN[:, sly * DM:(sly + 1) * DM], xq,
                        TS4f[:, 1:2], TS4f[:, 11:12], ALU.subtract, ALU.mult)
                wait("gpsimd", sDV, dv)
                y_st[s] = dma(
                    "gpsimd",
                    y[tb * QBT + s * 128: tb * QBT + (s + 1) * 128, :],
                    YN[:, sly * DM:(sly + 1) * DM], sem=sT[sly])
        barrier()
        es_pp.close()
        es_w2.close()

    wait("gpsimd", sST, sST.n)
    wait("gpsimd", sT[0], sT[0].n)
    wait("gpsimd", sT[1], sT[1].n)
    es_f.close()

    with nc.Block() as block:
        @block.sync
        def _(e):
            for f in prog["sync"]:
                f(e)

        @block.vector
        def _(e):
            for f in prog["vector"]:
                f(e)

        @block.scalar
        def _(e):
            for f in prog["scalar"]:
                f(e)

        @block.tensor
        def _(e):
            for f in prog["tensor"]:
                f(e)

        @block.gpsimd
        def _(e):
            for f in prog["gpsimd"]:
                f(e)

    es_glob.close()
    es_sem.close()
    return nc


def _host_constants():
    try:
        import jax
        import jax.numpy as jnp
        cpus = jax.devices("cpu")
        with jax.default_device(cpus[0]):
            inv = 1.0 / (1000000.0 ** (jnp.arange(0, HD, 2, dtype=jnp.float32)
                                       / HD))
            freqs = jnp.outer(jnp.arange(S, dtype=jnp.float32), inv)
            cos = np.asarray(jnp.cos(freqs))
            sin = np.asarray(jnp.sin(freqs))
    except Exception:
        inv = 1.0 / (1000000.0 ** (np.arange(0, HD, 2, dtype=np.float32)
                                   / np.float32(HD)))
        freqs = np.outer(np.arange(S, dtype=np.float32),
                         inv).astype(np.float32)
        cos = np.cos(freqs).astype(np.float32)
        sin = np.sin(freqs).astype(np.float32)
    c2 = np.empty((S, HD), np.float32)
    c2[:, 0::2] = cos
    c2[:, 1::2] = cos
    s2 = np.empty((S, HD), np.float32)
    s2[:, 0::2] = -sin
    s2[:, 1::2] = sin
    identf = np.eye(128, dtype=np.float32)
    mask = np.zeros((4, 128, QBT), ml_dtypes.bfloat16)
    for j in range(4):
        for p in range(128):
            mask[j, p, 128 * j + p:] = 1.0
    blk = np.zeros((128, 128), ml_dtypes.bfloat16)
    blk[:64, :64] = 1
    blk[64:, 64:] = 1
    return c2, s2, identf, mask, blk


def kernel(L0_attn_x, L0_attn_w_qkv, L0_attn_w_o, L0_attn_q_norm_w,
           L0_attn_q_norm_b, L0_attn_k_norm_w, L0_attn_k_norm_b,
           L0_ffn_w_gate_up, L0_ffn_w_proj):
    x = np.asarray(L0_attn_x, np.float32)
    wqkvgT = np.ascontiguousarray(np.asarray(L0_attn_w_qkv, np.float32).T)
    woT = np.ascontiguousarray(np.asarray(L0_attn_w_o, np.float32).T)
    wguT = np.ascontiguousarray(np.asarray(L0_ffn_w_gate_up, np.float32).T)
    wpT = np.ascontiguousarray(np.asarray(L0_ffn_w_proj, np.float32).T)
    qkn = np.stack([
        np.broadcast_to(np.asarray(L0_attn_q_norm_w, np.float32), (128, HD)),
        np.broadcast_to(np.asarray(L0_attn_q_norm_b, np.float32), (128, HD)),
        np.broadcast_to(np.asarray(L0_attn_k_norm_w, np.float32), (128, HD)),
        np.broadcast_to(np.asarray(L0_attn_k_norm_b, np.float32), (128, HD)),
    ]).copy()
    c2, s2, identf, mask, blk = _host_constants()

    nc = build()
    in_maps = []
    for b in range(8):
        in_maps.append({
            "xT": np.ascontiguousarray(x[b].T),
            "xR": np.ascontiguousarray(x[b]),
            "wqkvgT": wqkvgT, "woT": woT, "wguT": wguT, "wpT": wpT,
            "ropeC2": c2, "ropeS2": s2, "qkn": qkn,
            "identf": identf, "mask01": mask, "blkones": blk,
        })
    r = run_bass_kernel_spmd(nc, in_maps, core_ids=list(range(8)))
    return np.stack([np.asarray(rr["y"]) for rr in r.results], axis=0)


if __name__ == "__main__":
    build()
    print("built ok")


# version 15
# speedup vs baseline: 1.1153x; 1.0083x over previous
import math
import contextlib
import numpy as np
import ml_dtypes
import concourse.bass as bass
import concourse.mybir as mybir
from concourse.bass_utils import run_bass_kernel_spmd

F32 = mybir.dt.float32
BF16 = mybir.dt.bfloat16
AF = mybir.ActivationFunctionType
ALU = mybir.AluOpType
AX = mybir.AxisListType

S = 2048
DM = 2048
NH = 16
NKV = 4
HD = 128
FH = 8192
NQB = 4
QBT = 512
NTB = 4
TBT = 512
NCH = 16

LN_EPS = 1e-5
QK_EPS = 1e-6
RMS_EPS = 1.1920928955078125e-07
RSCALE = 1.0 / math.sqrt(128.0)
M_TIE = 0.501953125

ENGS = ("sync", "vector", "scalar", "tensor", "gpsimd")


def build(dbg=False):
    nc = bass.Bass("TRN2", target_bir_lowering=False, debug=False,
                   detect_race_conditions=False)

    xT = nc.dram_tensor("xT", [DM, S], F32, kind="ExternalInput").ap()
    xR = nc.dram_tensor("xR", [S, DM], F32, kind="ExternalInput").ap()
    wqkvgT = nc.dram_tensor("wqkvgT", [DM, 5120], F32, kind="ExternalInput").ap()
    woT = nc.dram_tensor("woT", [DM, DM], F32, kind="ExternalInput").ap()
    wguT = nc.dram_tensor("wguT", [DM, 2 * FH], F32, kind="ExternalInput").ap()
    wpT = nc.dram_tensor("wpT", [FH, DM], F32, kind="ExternalInput").ap()
    ropeC2 = nc.dram_tensor("ropeC2", [S, HD], F32, kind="ExternalInput").ap()
    ropeS2 = nc.dram_tensor("ropeS2", [S, HD], F32, kind="ExternalInput").ap()
    qkn = nc.dram_tensor("qkn", [4, 128, HD], F32, kind="ExternalInput").ap()
    identf = nc.dram_tensor("identf", [128, 128], F32, kind="ExternalInput").ap()
    mask01 = nc.dram_tensor("mask01", [4, 128, QBT], BF16, kind="ExternalInput").ap()
    blkones = nc.dram_tensor("blkones", [128, 128], BF16, kind="ExternalInput").ap()

    _scrkind = "ExternalOutput" if dbg else "Internal"
    x1scr = nc.dram_tensor("x1scr", [S, DM], F32, kind=_scrkind).ap()
    wkvgH = nc.dram_tensor("wkvgH", [DM, 5120], BF16, kind="Internal").ap()
    wkvgL = nc.dram_tensor("wkvgL", [DM, 5120], BF16, kind="Internal").ap()
    woH = nc.dram_tensor("woH", [DM, DM], BF16, kind="Internal").ap()
    woL = nc.dram_tensor("woL", [DM, DM], BF16, kind="Internal").ap()
    wqgscr = nc.dram_tensor("wqgscr", [128, 128, DM], BF16, kind=_scrkind).ap()
    wqpscr = nc.dram_tensor("wqpscr", [64, 128, DM], BF16, kind=_scrkind).ap()

    y = nc.dram_tensor("y", [S, DM], F32, kind="ExternalOutput").ap()
    if dbg:
        d_k = nc.dram_tensor("d_k", [2, 128, NKV * S], F32, kind="ExternalOutput").ap()
        d_v = nc.dram_tensor("d_v", [2, 128, NKV * S], F32, kind="ExternalOutput").ap()
        d_q = nc.dram_tensor("d_q", [2, 128, NH * QBT], F32, kind="ExternalOutput").ap()
        d_sg = nc.dram_tensor("d_sg", [128, NH * QBT], F32, kind="ExternalOutput").ap()
        d_g = nc.dram_tensor("d_g", [2, 128, NH * QBT], F32, kind="ExternalOutput").ap()

    prog = {e: [] for e in ENGS}

    def em(eng, fn):
        prog[eng].append(fn)

    es_sem = contextlib.ExitStack()

    class Ct:
        def __init__(self, name):
            self.h = es_sem.enter_context(nc.semaphore(name))
            self.n = 0

    sLD = Ct("sLD")
    sST = Ct("sST")
    sPE = Ct("sPE")
    sDV = Ct("sDV")
    sAC = Ct("sAC")
    sS = [Ct(f"sS{i}") for i in range(4)]
    sT = [Ct(f"sT{i}") for i in range(2)]
    sGP = Ct("sGP")
    ALLS = {"sLD": sLD, "sST": sST, "sPE": sPE, "sDV": sDV, "sAC": sAC,
            "sGP": sGP}
    for i, c in enumerate(sS):
        ALLS[f"sS{i}"] = c
    for i, c in enumerate(sT):
        ALLS[f"sT{i}"] = c
    ECT = {"sync": sLD, "gpsimd": sST, "vector": sDV, "scalar": sAC,
           "tensor": sPE}
    OPCT = dict(ECT, gpsimd=sGP)

    def wait(eng, ct, val):
        if val > 0:
            em(eng, lambda e, h=ct.h, v=val: e.wait_ge(h, v))

    def barrier():
        vals = {k: c.n for k, c in ALLS.items()}
        for eng in ENGS:
            for k, c in ALLS.items():
                wait(eng, c, vals[k])

    def dma(eng, out_ap, in_ap, sem=None):
        ct = sem if sem is not None else ECT[eng]
        em(eng, lambda e, o=out_ap, i=in_ap, h=ct.h:
           e.dma_start(out=o, in_=i).then_inc(h, 16))
        ct.n += 16
        return ct.n

    def op(eng, name, *args, inc=True, **kw):
        ct = OPCT[eng]
        need_drain = False
        if eng in ("vector", "scalar") and args:
            try:
                fb = args[0].free_nbytes()
                el4 = fb == 4 * args[0].free_size()
                need_drain = (fb <= 256 or args[0].partition_size() <= 2
                              or (el4 and fb <= 2048))
            except Exception:
                need_drain = False

        def f(e, name=name, args=args, kw=kw, inc=inc, h=ct.h, dr=need_drain):
            instr = getattr(e, name)(*args, **kw)
            if dr:
                e.drain()
            if inc:
                instr.then_inc(h, 1)
        em(eng, f)
        if inc:
            ct.n += 1
        return ct.n

    def pe_inc():
        em("tensor", lambda e, h=sPE.h: e.sem_inc(h, 1))
        sPE.n += 1
        return sPE.n

    es_glob = contextlib.ExitStack()

    _uid = [0]

    def _un(name):
        _uid[0] += 1
        return f"{name}_{_uid[0]}"

    def sb(es, name, shape, dt, side="left"):
        return es.enter_context(nc.sbuf_tensor(_un(name), shape, dt, side=side))

    def ps(es, name, shape, dt=F32):
        return es.enter_context(nc.psum_tensor(_un(name), shape, dt))

    IDF = sb(es_glob, "IDF", [128, 128], F32)
    BLK = sb(es_glob, "BLK", [128, 128], BF16)
    ONESK = sb(es_glob, "ONESK", [128, 1], BF16)
    ONES1 = sb(es_glob, "ONES1", [1, 128], BF16)
    ONESF = sb(es_glob, "ONESF", [128, 1], F32)

    CBIAS = sb(es_glob, "CBIAS", [128, 4], F32)
    nc.const_aps.register(F32, QK_EPS, CBIAS[:, 0:1])
    nc.const_aps.register(F32, LN_EPS, CBIAS[:, 1:2])
    nc.const_aps.register(F32, RMS_EPS, CBIAS[:, 2:3])

    dma("sync", IDF[:], identf)
    c_consts = dma("sync", BLK[:], blkones)
    op("vector", "memset", ONESK[:], 1.0, inc=False)
    op("vector", "memset", ONES1[:], 1.0, inc=False)
    op("vector", "memset", CBIAS[:, 0:1], QK_EPS, inc=False)
    op("vector", "memset", CBIAS[:, 1:2], LN_EPS, inc=False)
    op("vector", "memset", CBIAS[:, 2:3], RMS_EPS, inc=False)
    c_ones = op("vector", "memset", ONESF[:], 1.0)
    wait("scalar", sDV, c_ones)

    es_q = contextlib.ExitStack()
    NQS = 4
    QW = sb(es_q, "QW", [128, NQS * DM], F32)
    QA = sb(es_q, "QA", [128, NQS * DM], BF16)
    QB = sb(es_q, "QB", [128, DM], BF16)
    QM = sb(es_q, "QM", [128, DM], BF16)
    QSG = sb(es_q, "QSG", [128, NQS * DM], BF16)
    QT_ = sb(es_q, "QT", [128, DM], BF16)
    QWB = sb(es_q, "QWB", [128, DM], BF16)
    QD = sb(es_q, "QD", [128, DM], BF16)
    QH = sb(es_q, "QH", [128, NQS * DM], BF16)
    es_qp = contextlib.ExitStack()
    QP = [ps(es_qp, f"QP{j}", [128, 512]) for j in range(8)]

    qtiles = [("gu", c, fb) for c in range(NCH) for fb in range(8)]
    qtiles += [("p", c, 0) for c in range(64)]
    qtiles += [("kvg", c, fb) for c in range(NCH) for fb in range(3)]
    qtiles += [("wo", c, 0) for c in range(NCH)]

    q_ld, q_ab, q_mm, q_ev, q_st = {}, {}, {}, {}, {}
    q_free = {}
    wait("tensor", sLD, c_consts)
    for i, (kind, c, fb) in enumerate(qtiles):
        sl = i % NQS
        if i >= NQS:
            ctf, vf = q_free[i - NQS]
            wait("sync", ctf, vf)
        if kind == "gu":
            srcw = wguT[c * 128:(c + 1) * 128, fb * DM:(fb + 1) * DM]
            ncol = DM
        elif kind == "p":
            srcw = wpT[c * 128:(c + 1) * 128, 0:DM]
            ncol = DM
        elif kind == "kvg":
            w0 = fb * 1792
            ncol = min(1792, 5120 - w0)
            srcw = wqkvgT[c * 128:(c + 1) * 128, w0:w0 + ncol]
        else:
            srcw = woT[c * 128:(c + 1) * 128, 0:DM]
            ncol = DM
        q_ld[i] = dma("sync", QW[:, sl * DM: sl * DM + ncol], srcw,
                      sem=sS[sl % 4] if sl < 4 else sS[0])
        if kind in ("kvg", "wo"):
            wait("vector", sS[sl % 4], q_ld[i])
            if i >= NQS:
                wait("vector", sT[sl % 2], q_st[i - NQS])
            op("vector", "tensor_copy", QA[:, sl * DM: sl * DM + ncol],
               QW[:, sl * DM: sl * DM + ncol], inc=False)
            q_ev[i] = op("vector", "scalar_tensor_tensor",
                         QSG[:, sl * DM: sl * DM + ncol],
                         QA[:, sl * DM: sl * DM + ncol], -1.0,
                         QW[:, sl * DM: sl * DM + ncol], ALU.mult, ALU.add)
            q_free[i] = (sDV, q_ev[i])
            wait("gpsimd", sDV, q_ev[i])
            if kind == "kvg":
                dh = wkvgH[c * 128:(c + 1) * 128, w0:w0 + ncol]
                dl = wkvgL[c * 128:(c + 1) * 128, w0:w0 + ncol]
            else:
                dh = woH[c * 128:(c + 1) * 128, :]
                dl = woL[c * 128:(c + 1) * 128, :]
            dma("gpsimd", dh, QA[:, sl * DM: sl * DM + ncol], sem=sT[sl % 2])
            q_st[i] = dma("gpsimd", dl, QSG[:, sl * DM: sl * DM + ncol],
                          sem=sT[sl % 2])
            continue
        wait("scalar", sS[sl % 4], q_ld[i])
        if i >= NQS:
            wait("scalar", sPE, q_mm.get(i - NQS, 0))
            wait("scalar", sDV, q_ev[i - NQS])
        op("scalar", "activation", QA[:, sl * DM:(sl + 1) * DM],
           QW[:, sl * DM:(sl + 1) * DM], AF.Abs, inc=False)
        q_ab[i] = op("scalar", "activation", QSG[:, sl * DM:(sl + 1) * DM],
                     QW[:, sl * DM:(sl + 1) * DM], AF.Sign)
        q_free[i] = (sAC, q_ab[i])
        wait("tensor", sAC, q_ab[i])
        if i >= 2:
            wait("tensor", sDV, q_ev[i - 2])
        for j in range(4):
            q_mm[i] = op("tensor", "matmul", QP[((i % 2) * 4) + j][:], BLK[:],
                         QA[:, sl * DM + j * 512: sl * DM + (j + 1) * 512],
                         start=True, stop=True, inc=(j == 3))
        wait("vector", sPE, q_mm[i])
        if i >= NQS:
            wait("vector", sT[sl % 2], q_st[i - NQS])
        for j in range(4):
            op("vector", "tensor_scalar", QB[:, j * 512:(j + 1) * 512],
               QP[((i % 2) * 4) + j][:], 0.015625, 1e-8, ALU.mult, ALU.max,
               inc=False)
        op("vector", "scalar_tensor_tensor", QM[:], QB[:], M_TIE,
           QA[:, sl * DM:(sl + 1) * DM], ALU.mult, ALU.is_lt, inc=False)
        op("vector", "tensor_tensor", QT_[:], QM[:], QB[:], ALU.mult,
           inc=False)
        op("vector", "tensor_tensor", QD[:], QT_[:],
           QA[:, sl * DM:(sl + 1) * DM], ALU.subtract, inc=False)
        op("vector", "tensor_tensor", QWB[:],
           QA[:, sl * DM:(sl + 1) * DM], QD[:], ALU.add, inc=False)
        q_ev[i] = op("vector", "tensor_tensor", QH[:, sl * DM:(sl + 1) * DM],
                     QWB[:], QSG[:, sl * DM:(sl + 1) * DM], ALU.mult)
        wait("gpsimd", sDV, q_ev[i])
        if kind == "gu":
            dst = wqgscr[fb * 16:(fb + 1) * 16, :, c * 128:(c + 1) * 128]
            dst = dst.rearrange("hp p f -> p hp f")
            srcap = QH[:, sl * DM:(sl + 1) * DM].rearrange(
                "p (hp f) -> p hp f", hp=16)
        else:
            dst = wqpscr[c, :, :]
            srcap = QH[:, sl * DM:(sl + 1) * DM]
        q_st[i] = dma("gpsimd", dst, srcap, sem=sT[sl % 2])

    barrier()
    es_qp.close()
    es_q.close()

    es_a = contextlib.ExitStack()
    KH = sb(es_a, "KH", [128, NKV * S], BF16)
    KL = sb(es_a, "KL", [128, NKV * S], BF16)
    VH = sb(es_a, "VH", [128, NKV * S], BF16)
    VL = sb(es_a, "VL", [128, NKV * S], BF16)
    QKN = sb(es_a, "QKN", [128, 4 * HD], F32)
    MSK = sb(es_a, "MSK", [128, 4 * QBT], BF16)
    PCA = sb(es_a, "PCA", [128, 512], F32)
    TF0 = sb(es_a, "TF0", [128, 512], F32)
    TF1 = sb(es_a, "TF1", [128, 512], F32)
    TF2 = sb(es_a, "TF2", [128, 512], F32)
    TS4 = sb(es_a, "TS4", [128, 64], F32)

    dma("sync", QKN[:].rearrange("p (j f) -> p j f", j=4),
        qkn.rearrange("j p f -> p j f"))
    c_ac = dma("sync", MSK[:].rearrange("p (j f) -> p j f", j=4),
               mask01.rearrange("j p f -> p j f"))
    wait("vector", sLD, c_ac)
    wait("tensor", sLD, c_ac)
    wait("tensor", sDV, c_ones)

    FB_OFF = [0, 512, 1024, 1536, 2048, 2560, 3072, 3584, 4096, 4608]
    FB_KIND = ["q", "q", "q", "q", "g", "g", "g", "g", "k", "v"]

    class PairStream:

        def __init__(self, bh, bl, srch, srcl, sems):
            self.i = 0
            self.rdy = {}
            self.used = {}
            self.bh, self.bl = bh, bl
            self.srch, self.srcl = srch, srcl
            self.sems = sems

        def load(self, r0, c0, c1):
            bh, bl = self.bh, self.bl
            i = self.i
            self.i += 1
            sl = i % 2
            sem = self.sems[sl]
            if i >= 2:
                wait("sync", sPE, self.used[i - 2])
            w = c1 - c0
            srch = self.srch[r0:r0 + 512, c0:c1].rearrange(
                "(c p) f -> p c f", p=128)
            srcl = self.srcl[r0:r0 + 512, c0:c1].rearrange(
                "(c p) f -> p c f", p=128)
            dsth = bh[:, sl * 2048: sl * 2048 + 4 * w].rearrange(
                "p (c f) -> p c f", c=4)
            dstl = bl[:, sl * 2048: sl * 2048 + 4 * w].rearrange(
                "p (c f) -> p c f", c=4)
            dma("sync", dsth, srch, sem=sem)
            self.rdy[i] = dma("sync", dstl, srcl, sem=sem)
            return i, sl

        def chunk(self, sl, cc, w=512):
            return (self.bh[:, sl * 2048 + cc * w: sl * 2048 + (cc + 1) * w],
                    self.bl[:, sl * 2048 + cc * w: sl * 2048 + (cc + 1) * w])

        def consumed(self, i):
            self.used[i] = sPE.n

    for qb in range(NQB):
        es_l = contextlib.ExitStack()
        QTH = sb(es_l, "QTH", [128, NH * QBT], BF16)
        QTL = sb(es_l, "QTL", [128, NH * QBT], BF16)
        SGATE = sb(es_l, "SGATE", [128, NH * QBT], F32)
        es_p1 = contextlib.ExitStack()
        PW = [ps(es_p1, f"PW{j}", [128, 512]) for j in range(8)]
        es_a1 = contextlib.ExitStack()
        WSH = sb(es_a1, "WSH", [128, 2 * 2048], BF16)
        WSL = sb(es_a1, "WSL", [128, 2 * 2048], BF16)
        XT32 = sb(es_a1, "XT32", [128, 2048], F32)
        XTH = sb(es_a1, "XTH", [128, NCH * QBT], BF16)
        XTL = sb(es_a1, "XTL", [128, NCH * QBT], BF16)
        C2 = sb(es_a1, "C2", [128, 4 * HD], F32)
        S2T = sb(es_a1, "S2T", [128, 4 * HD], F32)
        XRP = sb(es_a1, "XRP", [128, 512], F32)

        dma("sync", C2[:].rearrange("p (tt f) -> p tt f", tt=4),
            ropeC2.rearrange("(tt p) f -> p tt f", p=128)[:, qb * 4:qb * 4 + 4, :])
        c_rope = dma("sync", S2T[:].rearrange("p (tt f) -> p tt f", tt=4),
                     ropeS2.rearrange("(tt p) f -> p tt f", p=128)[:, qb * 4:qb * 4 + 4, :])
        wait("vector", sLD, c_rope)

        ws = PairStream(WSH, WSL, wkvgH, wkvgL, (sS[0], sS[1]))
        xt_split = {}
        for cg in range(4):
            if cg >= 1:
                wait("sync", sDV, xt_split[cg - 1])
            srcx = xT[cg * 512:(cg + 1) * 512,
                      qb * QBT:(qb + 1) * QBT].rearrange(
                          "(c p) f -> p c f", p=128)
            dstx = XT32[:].rearrange("p (c f) -> p c f", c=4)
            t = dma("sync", dstx, srcx, sem=sS[2 + cg % 2])
            wait("vector", sS[2 + cg % 2], t)
            op("vector", "tensor_copy",
               XTH[:, cg * 4 * QBT:(cg + 1) * 4 * QBT],
               XT32[:], inc=False)
            xt_split[cg] = op("vector", "scalar_tensor_tensor",
                              XTL[:, cg * 4 * QBT:(cg + 1) * 4 * QBT],
                              XTH[:, cg * 4 * QBT:(cg + 1) * 4 * QBT], -1.0,
                              XT32[:], ALU.mult, ALU.add)
        xt_ready = xt_split[3]
        wave_free = [0, 0]
        grp_done = {}
        pend_evac = []

        def emit_mms(fb):
            kind = FB_KIND[fb]
            w = fb % 2
            banks = PW[w * 4:(w + 1) * 4]
            for c in range(NCH):
                cc = c % 4
                if cc == 0:
                    i, sl = ws.load(c * 128, FB_OFF[fb], FB_OFF[fb] + 512)
                    wait("tensor", ws.sems[sl], ws.rdy[i])
                if c == 0:
                    wait("tensor", sDV, wave_free[w])
                    wait("tensor", sDV, xt_ready)
                wsh, wsl = ws.chunk(sl, cc)
                xth_f = XTH[:, c * QBT:(c + 1) * QBT]
                xtl_f = XTL[:, c * QBT:(c + 1) * QBT]
                if kind == "g":
                    for fs in range(4):
                        op("tensor", "matmul", banks[fs][:],
                           wsh[:, fs * 128:(fs + 1) * 128], xth_f,
                           start=(c == 0), stop=False, inc=False)
                        op("tensor", "matmul", banks[fs][:],
                           wsh[:, fs * 128:(fs + 1) * 128], xtl_f,
                           start=False, stop=False, inc=False)
                        op("tensor", "matmul", banks[fs][:],
                           wsl[:, fs * 128:(fs + 1) * 128], xth_f,
                           start=False, stop=(c == NCH - 1), inc=(fs == 3))
                else:
                    for s in range(4):
                        xth = xth_f[:, s * 128:(s + 1) * 128]
                        xtl = xtl_f[:, s * 128:(s + 1) * 128]
                        op("tensor", "matmul", banks[s][:], xth, wsh,
                           start=(c == 0), stop=False, inc=False)
                        op("tensor", "matmul", banks[s][:], xth, wsl,
                           start=False, stop=False, inc=False)
                        op("tensor", "matmul", banks[s][:], xtl, wsh,
                           start=False, stop=(c == NCH - 1), inc=(s == 3))
                if cc == 3:
                    ws.consumed(i)
            grp_done[fb] = sPE.n

        def emit_evac(fb):
            kind = FB_KIND[fb]
            w = fb % 2
            banks = PW[w * 4:(w + 1) * 4]
            gd = grp_done[fb]
            last = 0
            if kind == "g":
                sgx_read = getattr(emit_evac, "_sgx_read", [0, 0])
                for fs in range(4):
                    hh = (fb - 4) * 4 + fs
                    wait("scalar", sPE, gd)
                    wait("scalar", sDV,
                         max(sgx_read[0], getattr(emit_evac, "_sqg", 0)))
                    ac = op("scalar", "activation",
                            TF2[:], banks[fs][:], AF.Exp, scale=-1.0)
                    wait("vector", sAC, ac)
                    sgx_read[0] = op("vector", "tensor_scalar", TF0[:],
                                     TF2[:], 1.0, None, ALU.add)
                    last = op("vector", "reciprocal",
                              SGATE[:, hh * QBT:(hh + 1) * QBT], TF0[:])
                emit_evac._sgx_read = sgx_read
            elif kind == "v":
                vh = VH[:].rearrange("p (kv tt f) -> p kv tt f", kv=NKV, tt=16)
                vl = VL[:].rearrange("p (kv tt f) -> p kv tt f", kv=NKV, tt=16)
                for s in range(4):
                    tt = qb * 4 + s
                    wait("vector", sPE, gd)
                    op("vector", "tensor_copy", PCA[:], banks[s][:],
                       inc=False)
                    srcv = PCA[:].rearrange("p (kv f) -> p kv f", kv=NKV)
                    op("vector", "tensor_copy", vh[:, :, tt, :], srcv,
                       inc=False)
                    last = op("vector", "scalar_tensor_tensor",
                              vl[:, :, tt, :], vh[:, :, tt, :], -1.0, srcv,
                              ALU.mult, ALU.add)
            else:
                goff = 0 if kind == "q" else 2
                gam = QKN[:, goff * HD:(goff + 1) * HD]
                bet = QKN[:, (goff + 1) * HD:(goff + 2) * HD]
                for s in range(4):
                    tt = qb * 4 + s
                    bank = banks[s]
                    xsl = s % 2
                    sq_guard = max(getattr(emit_evac, "_sqg", 0),
                                   getattr(emit_evac, "_sgx_read", [0, 0])[0])
                    wait("vector", sPE, gd)
                    pcd = op("vector", "tensor_copy", PCA[:], bank[:])
                    wait("scalar", sDV, pcd)
                    wait("scalar", sDV, sq_guard)
                    acq = op("scalar", "activation", TF2[:], PCA[:],
                             AF.Square)
                    wait("vector", sAC, acq)
                    op("vector", "tensor_reduce", TS4[:, 0:4],
                       PCA[:].rearrange("p (g n) -> p g n", g=4), AX.X,
                       ALU.add, inc=False)
                    op("vector", "tensor_scalar", TS4[:, 4:8], TS4[:, 0:4],
                       1.0 / 128.0, None, ALU.mult, inc=False)
                    op("vector", "tensor_reduce", TS4[:, 8:12],
                       TF2[:].rearrange("p (g n) -> p g n", g=4), AX.X,
                       ALU.add, inc=False)
                    op("vector", "tensor_scalar", TS4[:, 12:16], TS4[:, 8:12],
                       1.0 / 128.0, None, ALU.mult, inc=False)
                    op("vector", "tensor_tensor", TS4[:, 16:20], TS4[:, 4:8],
                       TS4[:, 4:8], ALU.mult, inc=False)
                    dv = op("vector", "tensor_tensor", TS4[:, 20:24],
                            TS4[:, 12:16], TS4[:, 16:20], ALU.subtract)
                    wait("scalar", sDV, dv)
                    op("scalar", "activation", TS4[:, 24:28], TS4[:, 20:24],
                       AF.Ln, bias=QK_EPS, inc=False)
                    ac = op("scalar", "activation", TS4[:, 28:32],
                            TS4[:, 24:28], AF.Exp, scale=-0.5)
                    wait("vector", sAC, ac)
                    pcv = PCA[:].rearrange("p (g f) -> p g f", g=4)
                    op("vector", "tensor_tensor",
                       TF1[:].rearrange("p (g f) -> p g f", g=4), pcv,
                       TS4[:, 4:8].unsqueeze(2).broadcast_to((128, 4, HD)),
                       ALU.subtract, inc=False)
                    op("vector", "tensor_tensor",
                       TF1[:].rearrange("p (g f) -> p g f", g=4),
                       TF1[:].rearrange("p (g f) -> p g f", g=4),
                       TS4[:, 28:32].unsqueeze(2).broadcast_to((128, 4, HD)),
                       ALU.mult, inc=False)
                    t1v = TF1[:].rearrange("p (g f) -> p g f", g=4)
                    t2v = TF2[:].rearrange("p (g f) -> p g f", g=4)
                    op("vector", "tensor_tensor", t2v, t1v,
                       gam.unsqueeze(1).broadcast_to((128, 4, HD)), ALU.mult,
                       inc=False)
                    op("vector", "tensor_tensor", t1v, t2v,
                       bet.unsqueeze(1).broadcast_to((128, 4, HD)), ALU.add,
                       inc=False)
                    if True:
                        pass
                    xrp_tr = getattr(emit_evac, "_xrp_tr", [0, 0])
                    wait("vector", sPE, xrp_tr[0])
                    xr = XRP[:]
                    c2b = C2[:, s * HD:(s + 1) * HD].unsqueeze(1)
                    c2b = c2b.broadcast_to((128, 4, HD))
                    s2b = S2T[:, s * HD:(s + 1) * HD].rearrange(
                        "p (pr two) -> p pr two", two=2).unsqueeze(1)
                    s2b = s2b.broadcast_to((128, 4, 64, 2))
                    op("vector", "tensor_tensor", TF0[:].rearrange(
                        "p (g f) -> p g f", g=4), t1v, c2b, ALU.mult,
                       inc=False)
                    swap = TF1[:].rearrange("p (g pr two) -> p g pr two",
                                            g=4, two=2)[:, :, :, ::-1]
                    op("vector", "tensor_tensor", TF2[:].rearrange(
                        "p (g pr two) -> p g pr two", g=4, two=2), swap, s2b,
                       ALU.mult, inc=False)
                    rp = op("vector", "tensor_tensor", xr, TF0[:], TF2[:],
                            ALU.add)
                    emit_evac._sqg = rp
                    wait("tensor", sDV, rp)
                    for g in range(4):
                        op("tensor", "transpose",
                           bank[:, g * 128:(g + 1) * 128],
                           xr[:, g * 128:(g + 1) * 128], IDF[:],
                           inc=(g == 3))
                    trd = sPE.n
                    xrp_tr[0] = trd
                    emit_evac._xrp_tr = xrp_tr
                    wait("vector", sPE, trd)
                    if kind == "q":
                        dsth = QTH[:].rearrange("p (hh t) -> p hh t", hh=NH)
                        dstl = QTL[:].rearrange("p (hh t) -> p hh t", hh=NH)
                        dh = dsth[:, fb * 4:fb * 4 + 4, s * 128:(s + 1) * 128]
                        dl = dstl[:, fb * 4:fb * 4 + 4, s * 128:(s + 1) * 128]
                    else:
                        dsth = KH[:].rearrange("p (kv t) -> p kv t", kv=NKV)
                        dstl = KL[:].rearrange("p (kv t) -> p kv t", kv=NKV)
                        dh = dsth[:, :, qb * QBT + s * 128:
                                  qb * QBT + (s + 1) * 128]
                        dl = dstl[:, :, qb * QBT + s * 128:
                                  qb * QBT + (s + 1) * 128]
                    op("vector", "tensor_copy", PCA[:], bank[:], inc=False)
                    srcv = PCA[:].rearrange("p (g f) -> p g f", g=4)
                    op("vector", "tensor_copy", dh, srcv, inc=False)
                    last = op("vector", "scalar_tensor_tensor", dl, dh, -1.0,
                              srcv, ALU.mult, ALU.add)
            wave_free[w] = last

        for fb in range(10):
            emit_mms(fb)
            if fb >= 1:
                emit_evac(fb - 1)
        emit_evac(9)

        barrier()
        if dbg and qb == 0:
            dma("gpsimd", d_k[0], KH[:])
            dma("gpsimd", d_k[1], KL[:])
            dma("gpsimd", d_v[0], VH[:])
            dma("gpsimd", d_v[1], VL[:])
            dma("gpsimd", d_q[0], QTH[:])
            dma("gpsimd", d_q[1], QTL[:])
            dma("gpsimd", d_sg, SGATE[:])
            wait("gpsimd", sST, sST.n)
            barrier()
        es_p1.close()
        es_a1.close()

        es_r2 = contextlib.ExitStack()
        GH = sb(es_r2, "GH", [128, NH * QBT], BF16, side="right")
        GL = sb(es_r2, "GL", [128, NH * QBT], BF16, side="right")
        EF = sb(es_r2, "EF", [128, 2 * 512], F32, side="right")
        EHB = sb(es_r2, "EHB", [128, 2 * 512], BF16, side="right")
        ELB = sb(es_r2, "ELB", [128, 2 * 512], BF16, side="right")
        RD = sb(es_r2, "RD", [1, 512], F32, side="right")
        RDH = sb(es_r2, "RDH", [1, 512], BF16, side="right")
        RDL = sb(es_r2, "RDL", [1, 512], BF16, side="right")
        RB = sb(es_r2, "RB", [128, 512], F32, side="right")
        es_p2 = contextlib.ExitStack()
        PS_ = [ps(es_p2, "PSa", [128, 512]), ps(es_p2, "PSb", [128, 512])]
        PAV = ps(es_p2, "PAV", [128, 512])
        PDN = ps(es_p2, "PDN", [1, 512])
        PRB = ps(es_p2, "PRB", [128, 512])

        nk = 4 * (qb + 1)
        gate_done = 0
        for h in range(NH):
            kvh = h // 4
            qh_ap = QTH[:, h * QBT:(h + 1) * QBT]
            ql_ap = QTL[:, h * QBT:(h + 1) * QBT]
            exp_done, el_done, av_done = {}, {}, {}

            def emit_S(kc):
                sbk = kc % 2
                if kc >= 2:
                    wait("tensor", sAC, exp_done[kc - 2])
                kh = KH[:, kvh * S + kc * 128: kvh * S + (kc + 1) * 128]
                kl = KL[:, kvh * S + kc * 128: kvh * S + (kc + 1) * 128]
                op("tensor", "matmul", PS_[sbk][:], kh, qh_ap,
                   start=True, stop=False, inc=False)
                op("tensor", "matmul", PS_[sbk][:], kh, ql_ap,
                   start=False, stop=False, inc=False)
                op("tensor", "matmul", PS_[sbk][:], kl, qh_ap,
                   start=False, stop=True, inc=True)
                wait("scalar", sPE, sPE.n)
                if kc >= 2:
                    wait("scalar", sDV, el_done[kc - 2])
                diag = kc >= 4 * qb
                if diag:
                    if kc >= 2:
                        wait("scalar", sDV, el_done[kc - 2])
                    exp_done[kc] = op("scalar", "activation",
                                      EF[:, sbk * 512:(sbk + 1) * 512],
                                      PS_[sbk][:], AF.Exp, scale=RSCALE)
                else:
                    if kc >= 2:
                        wait("scalar", sPE, av_done[kc - 2])
                        wait("scalar", sDV, el_done[kc - 2])
                    op("scalar", "activation",
                       EF[:, sbk * 512:(sbk + 1) * 512],
                       PS_[sbk][:], AF.Exp, scale=RSCALE, inc=False)
                    exp_done[kc] = op("scalar", "activation",
                                      EHB[:, sbk * 512:(sbk + 1) * 512],
                                      PS_[sbk][:], AF.Exp, scale=RSCALE)
                wait("vector", sAC, exp_done[kc])
                if kc >= 2:
                    wait("vector", sPE, av_done[kc - 2])
                efs = EF[:, sbk * 512:(sbk + 1) * 512]
                if diag:
                    j = kc - 4 * qb
                    op("vector", "tensor_tensor", efs, efs,
                       MSK[:, j * QBT:(j + 1) * QBT], ALU.mult, inc=False)
                    op("vector", "tensor_copy",
                       EHB[:, sbk * 512:(sbk + 1) * 512], efs, inc=False)
                el_done[kc] = op("vector", "scalar_tensor_tensor",
                                 ELB[:, sbk * 512:(sbk + 1) * 512],
                                 EHB[:, sbk * 512:(sbk + 1) * 512], -1.0, efs,
                                 ALU.mult, ALU.add)

            def emit_AV(kc):
                sbk = kc % 2
                wait("tensor", sDV, el_done[kc])
                if kc == 0 and gate_done:
                    wait("tensor", sDV, gate_done)
                vh = VH[:, (kvh * 16 + kc) * 128:(kvh * 16 + kc + 1) * 128]
                vl = VL[:, (kvh * 16 + kc) * 128:(kvh * 16 + kc + 1) * 128]
                ehs = EHB[:, sbk * 512:(sbk + 1) * 512]
                els = ELB[:, sbk * 512:(sbk + 1) * 512]
                op("tensor", "matmul", PAV[:], vh, ehs,
                   start=(kc == 0), stop=False, inc=False)
                op("tensor", "matmul", PAV[:], vh, els,
                   start=False, stop=False, inc=False)
                op("tensor", "matmul", PAV[:], vl, ehs,
                   start=False, stop=(kc == nk - 1), inc=False)
                op("tensor", "matmul", PDN[:], ONESK[:], ehs,
                   start=(kc == 0), stop=False, inc=False)
                op("tensor", "matmul", PDN[:], ONESK[:], els,
                   start=False, stop=(kc == nk - 1), inc=True)
                av_done[kc] = sPE.n

            emit_S(0)
            for kc in range(nk):
                if kc + 1 < nk:
                    emit_S(kc + 1)
                emit_AV(kc)
            wait("vector", sPE, av_done[nk - 1])
            op("vector", "reciprocal", RD[:], PDN[:], inc=False)
            op("vector", "tensor_copy", RDH[:], RD[:], inc=False)
            dvv = op("vector", "scalar_tensor_tensor", RDL[:], RDH[:], -1.0,
                     RD[:], ALU.mult, ALU.add)
            wait("tensor", sDV, dvv)
            op("tensor", "matmul", PRB[:], ONES1[:], RDH[:],
               start=True, stop=False, inc=False)
            op("tensor", "matmul", PRB[:], ONES1[:], RDL[:],
               start=False, stop=True, inc=True)
            wait("vector", sPE, sPE.n)
            op("vector", "tensor_copy", RB[:], PRB[:], inc=False)
            op("vector", "tensor_tensor", TF0[:], PAV[:], RB[:], ALU.mult,
               inc=False)
            op("vector", "tensor_tensor", TF1[:], TF0[:],
               SGATE[:, h * QBT:(h + 1) * QBT], ALU.mult, inc=False)
            op("vector", "tensor_copy", GH[:, h * QBT:(h + 1) * QBT], TF1[:],
               inc=False)
            gate_done = op("vector", "scalar_tensor_tensor",
                           GL[:, h * QBT:(h + 1) * QBT],
                           GH[:, h * QBT:(h + 1) * QBT], -1.0, TF1[:],
                           ALU.mult, ALU.add)

        barrier()
        if dbg and qb == 0:
            dma("gpsimd", d_g[0], GH[:])
            dma("gpsimd", d_g[1], GL[:])
            wait("gpsimd", sST, sST.n)
            barrier()
        es_p2.close()
        es_l.close()

        es_l2 = contextlib.ExitStack()
        X1Q = sb(es_l2, "X1Q", [128, 4 * DM], F32)
        WSH = sb(es_l2, "WSHb", [128, 2 * 2048], BF16)
        WSL = sb(es_l2, "WSLb", [128, 2 * 2048], BF16)
        XRES = sb(es_l2, "XRES", [128, 2 * 512], F32)
        X1N = sb(es_l2, "X1N", [128, DM], F32)
        es_p3 = contextlib.ExitStack()
        PWO = [ps(es_p3, f"PWO{j}", [128, 512]) for j in range(8)]
        ws = PairStream(WSH, WSL, woH, woL, (sS[0], sS[1]))
        wo_wave_free = [0, 0]
        wo_grp = {}
        xres_used = [0, 0]

        def emit_wo_mms(db):
            w = db % 2
            banks = PWO[w * 4:(w + 1) * 4]
            for fc in range(NCH):
                cc = fc % 4
                if cc == 0:
                    i, sl = ws.load(fc * 128, db * 512, (db + 1) * 512)
                    wait("tensor", ws.sems[sl], ws.rdy[i])
                if fc == 0:
                    wait("tensor", sDV, wo_wave_free[w])
                wsh, wsl = ws.chunk(sl, cc)
                for s in range(4):
                    gh = GH[:, fc * QBT + s * 128: fc * QBT + (s + 1) * 128]
                    gl = GL[:, fc * QBT + s * 128: fc * QBT + (s + 1) * 128]
                    op("tensor", "matmul", banks[s][:], gh, wsh,
                       start=(fc == 0), stop=False, inc=False)
                    op("tensor", "matmul", banks[s][:], gh, wsl,
                       start=False, stop=False, inc=False)
                    op("tensor", "matmul", banks[s][:], gl, wsh,
                       start=False, stop=(fc == NCH - 1), inc=(s == 3))
                if cc == 3:
                    ws.consumed(i)
            wo_grp[db] = sPE.n

        def emit_wo_evac(db):
            nonlocal wo_wave_free, xres_used
            w = db % 2
            banks = PWO[w * 4:(w + 1) * 4]
            for s in range(4):
                sl = (db * 4 + s) % 2
                wait("sync", sDV, xres_used[sl])
                t = dma("sync", XRES[:, sl * 512:(sl + 1) * 512],
                        xR[qb * QBT + s * 128: qb * QBT + (s + 1) * 128,
                           db * 512:(db + 1) * 512], sem=sS[2 + sl])
                wait("vector", sS[2 + sl], t)
                wait("vector", sPE, wo_grp[db])
                ev = op("vector", "tensor_tensor",
                        X1Q[:, s * DM + db * 512: s * DM + (db + 1) * 512],
                        banks[s][:], XRES[:, sl * 512:(sl + 1) * 512],
                        ALU.add)
                xres_used[sl] = ev
            wo_wave_free[w] = ev

        for db in range(4):
            emit_wo_mms(db)
            if db >= 1:
                emit_wo_evac(db - 1)
        emit_wo_evac(3)

        x1_st = {}
        for s in range(4):
            xq = X1Q[:, s * DM:(s + 1) * DM]
            op("vector", "tensor_reduce", TS4[:, 0:1], xq.unsqueeze(1),
               AX.X, ALU.add, inc=False)
            op("vector", "tensor_scalar", TS4[:, 1:2], TS4[:, 0:1],
               1.0 / DM, None, ALU.mult, inc=False)
            for jj in range(4):
                op("vector", "scalar_tensor_tensor", TF0[:],
                   xq[:, jj * 512:(jj + 1) * 512], 1.0,
                   xq[:, jj * 512:(jj + 1) * 512], ALU.mult, ALU.mult,
                   accum_out=TS4[:, 2 + jj:3 + jj], inc=False)
            op("vector", "tensor_reduce", TS4[:, 6:7],
               TS4[:, 2:6].unsqueeze(1), AX.X, ALU.add, inc=False)
            op("vector", "tensor_scalar", TS4[:, 7:8], TS4[:, 6:7],
               1.0 / DM, None, ALU.mult, inc=False)
            op("vector", "tensor_tensor", TS4[:, 8:9], TS4[:, 1:2],
               TS4[:, 1:2], ALU.mult, inc=False)
            dv = op("vector", "tensor_tensor", TS4[:, 9:10], TS4[:, 7:8],
                    TS4[:, 8:9], ALU.subtract)
            wait("scalar", sDV, dv)
            op("scalar", "activation", TS4[:, 10:11], TS4[:, 9:10],
               AF.Ln, bias=LN_EPS, inc=False)
            ac = op("scalar", "activation", TS4[:, 11:12], TS4[:, 10:11],
                    AF.Exp, scale=-0.5)
            wait("vector", sAC, ac)
            if s >= 1:
                wait("vector", sST, x1_st[s - 1])
            dv = op("vector", "tensor_scalar", X1N[:],
                    xq, TS4[:, 1:2], TS4[:, 11:12], ALU.subtract, ALU.mult)
            wait("gpsimd", sDV, dv)
            x1_st[s] = dma(
                "gpsimd",
                x1scr[qb * QBT + s * 128: qb * QBT + (s + 1) * 128, :],
                X1N[:])
        barrier()
        es_p3.close()
        es_l2.close()
        es_r2.close()

    barrier()
    es_a.close()

    es_f = contextlib.ExitStack()
    TF0f = sb(es_f, "TF0f", [128, 2 * 512], F32)
    TF1f = sb(es_f, "TF1f", [128, 512], F32)
    TF2f = sb(es_f, "TF2f", [128, 2 * 512], F32)
    TS4f = sb(es_f, "TS4f", [128, 64], F32)
    RSTB = sb(es_f, "RSTB", [128, 4], F32)
    LNR = sb(es_f, "LNR", [1, 2 * 512], F32)
    HH = sb(es_f, "HH", [128, 64 * TBT], BF16)
    HL = sb(es_f, "HL", [128, 64 * TBT], BF16)

    for tb in range(NTB):
        es_x = contextlib.ExitStack()
        X1TH = sb(es_x, "X1TH", [128, NCH * TBT], BF16, side="right")
        X1TL = sb(es_x, "X1TL", [128, NCH * TBT], BF16, side="right")
        X1LD = sb(es_x, "X1LD", [128, 2 * DM], F32, side="right")
        es_pt = contextlib.ExitStack()
        PTR = [ps(es_pt, f"PTR{j}", [128, 512]) for j in range(2)]
        tr_ev = [0, 0]
        s_trdone = {}
        for s in range(4):
            sl = s % 2
            if s >= 2:
                wait("sync", sPE, s_trdone[s - 2])
            t = dma("sync", X1LD[:, sl * DM:(sl + 1) * DM],
                    x1scr[tb * QBT + s * 128: tb * QBT + (s + 1) * 128, :],
                    sem=sS[sl])
            for grp in range(4):
                bk = grp % 2
                bank = PTR[bk]
                wait("tensor", sS[sl], t)
                wait("tensor", sDV, tr_ev[bk])
                for g in range(4):
                    c = grp * 4 + g
                    op("tensor", "transpose", bank[:, g * 128:(g + 1) * 128],
                       X1LD[:, sl * DM + c * 128: sl * DM + (c + 1) * 128],
                       IDF[:], inc=(g == 3))
                s_trdone[s] = sPE.n
                wait("vector", sPE, sPE.n)
                dsth = X1TH[:].rearrange("p (c t) -> p c t", c=NCH)
                dstl = X1TL[:].rearrange("p (c t) -> p c t", c=NCH)
                dh = dsth[:, grp * 4:(grp + 1) * 4, s * 128:(s + 1) * 128]
                dl = dstl[:, grp * 4:(grp + 1) * 4, s * 128:(s + 1) * 128]
                op("vector", "tensor_copy", TF0f[:, 0:512], bank[:],
                   inc=False)
                srcv = TF0f[:, 0:512].rearrange("p (g f) -> p g f", g=4)
                op("vector", "tensor_copy", dh, srcv, inc=False)
                tr_ev[bk] = op("vector", "scalar_tensor_tensor", dl, dh,
                               -1.0, srcv, ALU.mult, ALU.add)
        barrier()
        es_pt.close()

        es_w = contextlib.ExitStack()
        WQG = sb(es_w, "WQG", [128, 2 * DM], BF16, side="right")
        WQU = sb(es_w, "WQU", [128, 2 * DM], BF16, side="right")
        es_pg = contextlib.ExitStack()
        PG = [ps(es_pg, f"PG{j}", [128, 512]) for j in range(4)]
        PSS = ps(es_pg, "PSS", [1, 512])
        gu_grp = {}
        gu_ld = {}
        gu_ev = [0, 0]
        hsq_done = {}
        ss_mm = {}

        def emit_gu_mms(hp):
            sl = hp % 2
            w = hp % 2
            if hp >= 2:
                wait("sync", sPE, gu_ld[hp - 2])
            dma("sync", WQG[:, sl * DM:(sl + 1) * DM], wqgscr[hp, :, :],
                sem=sS[sl])
            t2 = dma("sync", WQU[:, sl * DM:(sl + 1) * DM],
                     wqgscr[64 + hp, :, :], sem=sS[sl])
            pg, pu = PG[w * 2], PG[w * 2 + 1]
            wait("tensor", sS[sl], t2)
            wait("tensor", sDV, gu_ev[w])
            for c in range(NCH):
                wg = WQG[:, sl * DM + c * 128: sl * DM + (c + 1) * 128]
                wu = WQU[:, sl * DM + c * 128: sl * DM + (c + 1) * 128]
                xh = X1TH[:, c * TBT:(c + 1) * TBT]
                xl = X1TL[:, c * TBT:(c + 1) * TBT]
                op("tensor", "matmul", pg[:], wg, xh,
                   start=(c == 0), stop=False, inc=False)
                op("tensor", "matmul", pg[:], wg, xl,
                   start=False, stop=(c == NCH - 1), inc=False)
                op("tensor", "matmul", pu[:], wu, xh,
                   start=(c == 0), stop=False, inc=False)
                op("tensor", "matmul", pu[:], wu, xl,
                   start=False, stop=(c == NCH - 1), inc=(c == NCH - 1))
            gu_ld[hp] = sPE.n
            gu_grp[hp] = sPE.n

        def emit_gu_evac(hp):
            w = hp % 2
            pg, pu = PG[w * 2], PG[w * 2 + 1]
            wait("scalar", sPE, gu_grp[hp])
            if hp >= 2:
                wait("scalar", sDV, hsq_done[hp - 2])
            ac = op("scalar", "activation", TF0f[:, w * 512:(w + 1) * 512],
                    pg[:], AF.Silu)
            wait("vector", sAC, ac)
            if hp >= 2:
                wait("vector", sPE, ss_mm.get(hp - 2, 0))
            op("vector", "tensor_tensor", TF1f[:],
               TF0f[:, w * 512:(w + 1) * 512], pu[:], ALU.mult, inc=False)
            op("vector", "tensor_tensor", TF2f[:, w * 512:(w + 1) * 512],
               TF1f[:], TF1f[:], ALU.mult, inc=False)
            op("vector", "tensor_copy", HH[:, hp * TBT:(hp + 1) * TBT],
               TF1f[:], inc=False)
            hsq_done[hp] = op("vector", "scalar_tensor_tensor",
                              HL[:, hp * TBT:(hp + 1) * TBT],
                              HH[:, hp * TBT:(hp + 1) * TBT], -1.0, TF1f[:],
                              ALU.mult, ALU.add)
            gu_ev[w] = hsq_done[hp]
            wait("tensor", sDV, hsq_done[hp])
            ss_mm[hp] = op("tensor", "matmul", PSS[:], ONESF[:],
                           TF2f[:, w * 512:(w + 1) * 512],
                           start=(hp == 0), stop=(hp == 63), inc=True)

        for hp in range(64):
            emit_gu_mms(hp)
            if hp >= 1:
                emit_gu_evac(hp - 1)
        emit_gu_evac(63)

        wait("scalar", sPE, ss_mm[63])
        op("scalar", "activation", LNR[:, 0:512], PSS[:], AF.Ln,
           scale=1.0 / FH, bias=RMS_EPS, inc=False)
        ac = op("scalar", "activation", LNR[:, 512:1024], LNR[:, 0:512],
                AF.Exp, scale=-0.5)
        wait("tensor", sAC, ac)
        for g in range(4):
            op("tensor", "transpose", PG[0][:, g:g + 1],
               LNR[0:1, 512 + g * 128: 512 + (g + 1) * 128], IDF[0:1, 0:1],
               inc=(g == 3))
        wait("vector", sPE, sPE.n)
        rst_ev = op("vector", "tensor_copy", RSTB[:], PG[0][:, 0:4])
        barrier()
        es_pg.close()
        es_w.close()
        es_x.close()

        es_w2 = contextlib.ExitStack()
        WQP = sb(es_w2, "WQP", [128, 2 * DM], BF16, side="right")
        X2Q = sb(es_w2, "X2Q", [128, 4 * DM], F32, side="right")
        XR2 = sb(es_w2, "XR2", [128, 2 * 512], F32, side="right")
        YN = sb(es_w2, "YN", [128, 2 * DM], F32, side="right")
        es_pp = contextlib.ExitStack()
        PP = [ps(es_pp, f"PP{j}", [128, 512]) for j in range(8)]
        y_st = {}
        x2_ev = 0
        xr2_used = [0, 0]
        for half in range(2):
            subs = (0, 1) if half == 0 else (2, 3)
            wqp_ld = {}
            for c in range(64):
                sl = c % 2
                if c >= 2:
                    wait("sync", sPE, wqp_ld[c - 2])
                t = dma("sync", WQP[:, sl * DM:(sl + 1) * DM], wqpscr[c, :, :],
                        sem=sS[sl])
                wait("tensor", sS[sl], t)
                if c == 0 and half == 1:
                    wait("tensor", sDV, x2_ev)
                for si, s in enumerate(subs):
                    hh = HH[:, c * TBT + s * 128: c * TBT + (s + 1) * 128]
                    hl = HL[:, c * TBT + s * 128: c * TBT + (s + 1) * 128]
                    for db in range(4):
                        wq = WQP[:, sl * DM + db * 512:
                                 sl * DM + (db + 1) * 512]
                        bank = PP[si * 4 + db]
                        op("tensor", "matmul", bank[:], hh, wq,
                           start=(c == 0), stop=False, inc=False)
                        op("tensor", "matmul", bank[:], hl, wq,
                           start=False, stop=(c == 63),
                           inc=(si == len(subs) - 1 and db == 3))
                wqp_ld[c] = sPE.n
            grp = sPE.n
            for si, s in enumerate(subs):
                for db in range(4):
                    sl2 = db % 2
                    wait("sync", sDV, xr2_used[sl2])
                    t = dma("sync", XR2[:, sl2 * 512:(sl2 + 1) * 512],
                            x1scr[tb * QBT + s * 128: tb * QBT + (s + 1) * 128,
                                  db * 512:(db + 1) * 512], sem=sS[2 + sl2])
                    wait("vector", sS[2 + sl2], t)
                    wait("vector", sPE, grp)
                    wait("vector", sDV, rst_ev)
                    x2_ev = op("vector", "scalar_tensor_tensor",
                               X2Q[:, si * DM + db * 512:
                                   si * DM + (db + 1) * 512],
                               PP[si * 4 + db][:], RSTB[:, s:s + 1],
                               XR2[:, sl2 * 512:(sl2 + 1) * 512],
                               ALU.mult, ALU.add)
                    xr2_used[sl2] = x2_ev
            for si, s in enumerate(subs):
                xq = X2Q[:, si * DM:(si + 1) * DM]
                op("vector", "tensor_reduce", TS4f[:, 0:1], xq.unsqueeze(1),
                   AX.X, ALU.add, inc=False)
                op("vector", "tensor_scalar", TS4f[:, 1:2], TS4f[:, 0:1],
                   1.0 / DM, None, ALU.mult, inc=False)
                for jj in range(4):
                    op("vector", "scalar_tensor_tensor", TF0f[:, 0:512],
                       xq[:, jj * 512:(jj + 1) * 512], 1.0,
                       xq[:, jj * 512:(jj + 1) * 512], ALU.mult, ALU.mult,
                       accum_out=TS4f[:, 2 + jj:3 + jj], inc=False)
                op("vector", "tensor_reduce", TS4f[:, 6:7],
                   TS4f[:, 2:6].unsqueeze(1), AX.X, ALU.add, inc=False)
                op("vector", "tensor_scalar", TS4f[:, 7:8], TS4f[:, 6:7],
                   1.0 / DM, None, ALU.mult, inc=False)
                op("vector", "tensor_tensor", TS4f[:, 8:9], TS4f[:, 1:2],
                   TS4f[:, 1:2], ALU.mult, inc=False)
                dv = op("vector", "tensor_tensor", TS4f[:, 9:10],
                        TS4f[:, 7:8], TS4f[:, 8:9], ALU.subtract)
                wait("scalar", sDV, dv)
                op("scalar", "activation", TS4f[:, 10:11], TS4f[:, 9:10],
                   AF.Ln, bias=LN_EPS, inc=False)
                ac = op("scalar", "activation", TS4f[:, 11:12],
                        TS4f[:, 10:11], AF.Exp, scale=-0.5)
                wait("vector", sAC, ac)
                sly = s % 2
                if s >= 2:
                    wait("vector", sT[sly], y_st[s - 2])
                dv = op("vector", "tensor_scalar",
                        YN[:, sly * DM:(sly + 1) * DM], xq,
                        TS4f[:, 1:2], TS4f[:, 11:12], ALU.subtract, ALU.mult)
                wait("gpsimd", sDV, dv)
                y_st[s] = dma(
                    "gpsimd",
                    y[tb * QBT + s * 128: tb * QBT + (s + 1) * 128, :],
                    YN[:, sly * DM:(sly + 1) * DM], sem=sT[sly])
        barrier()
        es_pp.close()
        es_w2.close()

    wait("gpsimd", sST, sST.n)
    wait("gpsimd", sT[0], sT[0].n)
    wait("gpsimd", sT[1], sT[1].n)
    es_f.close()

    with nc.Block() as block:
        @block.sync
        def _(e):
            for f in prog["sync"]:
                f(e)

        @block.vector
        def _(e):
            for f in prog["vector"]:
                f(e)

        @block.scalar
        def _(e):
            for f in prog["scalar"]:
                f(e)

        @block.tensor
        def _(e):
            for f in prog["tensor"]:
                f(e)

        @block.gpsimd
        def _(e):
            for f in prog["gpsimd"]:
                f(e)

    es_glob.close()
    es_sem.close()
    return nc


def _host_constants():
    try:
        import jax
        import jax.numpy as jnp
        cpus = jax.devices("cpu")
        with jax.default_device(cpus[0]):
            inv = 1.0 / (1000000.0 ** (jnp.arange(0, HD, 2, dtype=jnp.float32)
                                       / HD))
            freqs = jnp.outer(jnp.arange(S, dtype=jnp.float32), inv)
            cos = np.asarray(jnp.cos(freqs))
            sin = np.asarray(jnp.sin(freqs))
    except Exception:
        inv = 1.0 / (1000000.0 ** (np.arange(0, HD, 2, dtype=np.float32)
                                   / np.float32(HD)))
        freqs = np.outer(np.arange(S, dtype=np.float32),
                         inv).astype(np.float32)
        cos = np.cos(freqs).astype(np.float32)
        sin = np.sin(freqs).astype(np.float32)
    c2 = np.empty((S, HD), np.float32)
    c2[:, 0::2] = cos
    c2[:, 1::2] = cos
    s2 = np.empty((S, HD), np.float32)
    s2[:, 0::2] = -sin
    s2[:, 1::2] = sin
    identf = np.eye(128, dtype=np.float32)
    mask = np.zeros((4, 128, QBT), ml_dtypes.bfloat16)
    for j in range(4):
        for p in range(128):
            mask[j, p, 128 * j + p:] = 1.0
    blk = np.zeros((128, 128), ml_dtypes.bfloat16)
    blk[:64, :64] = 1
    blk[64:, 64:] = 1
    return c2, s2, identf, mask, blk


def kernel(L0_attn_x, L0_attn_w_qkv, L0_attn_w_o, L0_attn_q_norm_w,
           L0_attn_q_norm_b, L0_attn_k_norm_w, L0_attn_k_norm_b,
           L0_ffn_w_gate_up, L0_ffn_w_proj):
    x = np.asarray(L0_attn_x, np.float32)
    wqkvgT = np.ascontiguousarray(np.asarray(L0_attn_w_qkv, np.float32).T)
    woT = np.ascontiguousarray(np.asarray(L0_attn_w_o, np.float32).T)
    wguT = np.ascontiguousarray(np.asarray(L0_ffn_w_gate_up, np.float32).T)
    wpT = np.ascontiguousarray(np.asarray(L0_ffn_w_proj, np.float32).T)
    qkn = np.stack([
        np.broadcast_to(np.asarray(L0_attn_q_norm_w, np.float32), (128, HD)),
        np.broadcast_to(np.asarray(L0_attn_q_norm_b, np.float32), (128, HD)),
        np.broadcast_to(np.asarray(L0_attn_k_norm_w, np.float32), (128, HD)),
        np.broadcast_to(np.asarray(L0_attn_k_norm_b, np.float32), (128, HD)),
    ]).copy()
    c2, s2, identf, mask, blk = _host_constants()

    nc = build()
    in_maps = []
    for b in range(8):
        in_maps.append({
            "xT": np.ascontiguousarray(x[b].T),
            "xR": np.ascontiguousarray(x[b]),
            "wqkvgT": wqkvgT, "woT": woT, "wguT": wguT, "wpT": wpT,
            "ropeC2": c2, "ropeS2": s2, "qkn": qkn,
            "identf": identf, "mask01": mask, "blkones": blk,
        })
    r = run_bass_kernel_spmd(nc, in_maps, core_ids=list(range(8)))
    return np.stack([np.asarray(rr["y"]) for rr in r.results], axis=0)


if __name__ == "__main__":
    build()
    print("built ok")
